# version 07
# baseline (speedup 1.0000x reference)
import contextlib
import numpy as np
import concourse.bass as bass
import concourse.mybir as mybir
from concourse.alu_op_type import AluOpType

F32 = mybir.dt.float32
BF16 = mybir.dt.bfloat16
I8 = mybir.dt.int8
AF = mybir.ActivationFunctionType
AX = mybir.AxisListType.X

D = 2048
NT = 4096
KD = 16
CH = 256
NCH = 16
INTER = 8192
LN_EPS = 1e-5
RMS_EPS = float(np.finfo(np.float32).eps)

PHASES = ["A", "B", "C", "D", "E1", "E2"]


def build(upto="E2", debug=False):
    nc = bass.Bass("TRN2", target_bir_lowering=False)
    upto_i = PHASES.index(upto)

    x_in = nc.dram_tensor("x_in", [NT, D], F32, kind="ExternalInput").ap()
    wq_t = nc.dram_tensor("wq_t", [D, D], F32, kind="ExternalInput").ap()
    wk_t = nc.dram_tensor("wk_t", [D, D], F32, kind="ExternalInput").ap()
    wv_t = nc.dram_tensor("wv_t", [D, D], F32, kind="ExternalInput").ap()
    wg_t = nc.dram_tensor("wg_t", [D, D], F32, kind="ExternalInput").ap()
    wp_t = nc.dram_tensor("wp_t", [D, D], F32, kind="ExternalInput").ap()
    w_gate = nc.dram_tensor("w_gate", [D, INTER], I8, kind="ExternalInput").ap()
    w_up = nc.dram_tensor("w_up", [D, INTER], I8, kind="ExternalInput").ap()
    w_down = nc.dram_tensor("w_down", [INTER, D], I8, kind="ExternalInput").ap()
    s_gate = nc.dram_tensor("s_gate", [128, INTER // 128], F32, kind="ExternalInput").ap()
    s_up = nc.dram_tensor("s_up", [128, INTER // 128], F32, kind="ExternalInput").ap()
    s_down = nc.dram_tensor("s_down", [D], F32, kind="ExternalInput").ap()
    ident_in = nc.dram_tensor("ident_in", [128, 128], BF16, kind="ExternalInput").ap()
    ones_in = nc.dram_tensor("ones_in", [1, 128], F32, kind="ExternalInput").ap()
    masks_in = nc.dram_tensor("masks_in", [128, 512], F32, kind="ExternalInput").ap()

    out_d = nc.dram_tensor("out", [NT, D], F32, kind="ExternalOutput").ap()

    def scratch(name, shape, dtype):
        return nc.dram_tensor(name, shape, dtype,
                              kind="ExternalOutput" if debug else "Internal").ap()

    xnT_d = scratch("xnT_d", [D, NT], BF16)
    xn2T_d = scratch("xn2T_d", [D, NT], BF16)
    qT_d = scratch("qT_d", [D, NT], BF16)
    k_d = scratch("k_d", [NT, D], BF16)
    v_d = scratch("v_d", [NT, D], BF16)
    oT_d = scratch("oT_d", [D, NT], BF16)
    sout_d = scratch("sout_d", [NT, D], F32)
    fuse_d = scratch("fuse_d", [NT // 256, 128, INTER // 128, 256], BF16)

    xnT_v = xnT_d.rearrange("(k p) t -> p k t", p=128)
    xn2T_v = xn2T_d.rearrange("(k p) t -> p k t", p=128)
    qT_v = qT_d.rearrange("(k p) t -> p k t", p=128)
    oT_v = oT_d.rearrange("(k p) t -> p k t", p=128)
    k_v = k_d.rearrange("(j p) f -> p j f", p=128)
    v_v = v_d.rearrange("(j p) f -> p j f", p=128)
    x_v = x_in.rearrange("(j p) f -> p j f", p=128)
    out_v = out_d.rearrange("(j p) f -> p j f", p=128)
    sout_v = sout_d.rearrange("(j p) f -> p j f", p=128)
    wqt_v = wq_t.rearrange("(k p) f -> p k f", p=128)
    wkt_v = wk_t.rearrange("(k p) f -> p k f", p=128)
    wvt_v = wv_t.rearrange("(k p) f -> p k f", p=128)
    wgt_v = wg_t.rearrange("(k p) f -> p k f", p=128)
    wpt_v = wp_t.rearrange("(k p) f -> p k f", p=128)
    wgate_v = w_gate.rearrange("(k p) f -> p k f", p=128)
    wup_v = w_up.rearrange("(k p) f -> p k f", p=128)
    wdown_v = w_down.rearrange("(k p) f -> p k f", p=128)


    with contextlib.ExitStack() as _es:
        ident = _es.enter_context(nc.sbuf_tensor("ident", [128, 128], BF16))
        psum = _es.enter_context(nc.psum_tensor("psum", [128, 4096], F32))
        BAR = _es.enter_context(nc.semaphore("BAR"))
        sLDS = _es.enter_context(nc.semaphore("LDS"))
        sLDG = _es.enter_context(nc.semaphore("LDG"))
        sMM = _es.enter_context(nc.semaphore("MM"))
        sEVV = _es.enter_context(nc.semaphore("EVV"))
        sEVS = _es.enter_context(nc.semaphore("EVS"))
        sSTS = _es.enter_context(nc.semaphore("STS"))
        sLDK = _es.enter_context(nc.semaphore("LDK"))
        sLDV = _es.enter_context(nc.semaphore("LDV"))
        sLDF = _es.enter_context(nc.semaphore("LDF"))
        block = _es.enter_context(nc.Block())
        ctr = {"LDS": 0, "LDG": 0, "MM": 0, "EVV": 0, "EVS": 0, "STS": 0, "BAR": 0,
               "LDK": 0, "LDV": 0, "LDF": 0}

        def bank(b):
            return psum[:, b * 512:(b + 1) * 512]

        def region(b, off, w):
            return psum[:, b * 512 + off:b * 512 + off + w]

        def barrier():
            ctr["BAR"] += 5
            tgt = ctr["BAR"]
            snap = dict(ctr)

            def drain(eng, name):
                eng.wait_ge(sMM, snap["MM"])
                eng.wait_ge(sEVV, snap["EVV"])
                eng.wait_ge(sEVS, snap["EVS"])
                eng.wait_ge(sLDS, snap["LDS"])
                eng.wait_ge(sLDK, snap["LDK"])
                eng.wait_ge(sLDV, snap["LDV"])
                eng.wait_ge(sLDF, snap["LDF"])
                eng.wait_ge(sSTS, snap["STS"])
                eng.wait_ge(sLDG, snap["LDG"])

            for name in ["tensor", "vector", "scalar", "gpsimd", "sync"]:
                def mk(eng_name):
                    def body(eng):
                        drain(eng, eng_name)
                        eng.nop().then_inc(BAR, 1)
                        eng.wait_ge(BAR, tgt)
                    return body
                getattr(block, name)(mk(name))

        @block.sync
        def _(sy):
            sy.dma_start(out=ident[:], in_=ident_in[:]).then_inc(sLDS, 16)
        ctr["LDS"] += 16

        @block.tensor
        def _(t):
            t.wait_ge(sLDS, ctr["LDS"])

        _ab_es = contextlib.ExitStack()
        bw0 = _ab_es.enter_context(nc.sbuf_tensor("bw0", [128, KD, D], BF16))

        with (
            nc.sbuf_tensor("a_xb", [128, 2, D], F32) as a_xb,
            nc.sbuf_tensor("a_sq", [128, D], F32) as a_sq,
            nc.sbuf_tensor("a_xnbf", [128, 2, D], BF16) as a_xnbf,
            nc.sbuf_tensor("a_xn2bf", [128, 2, D], BF16) as a_xn2bf,
            nc.sbuf_tensor("a_st", [128, 12, 32], F32) as a_st,
            nc.sbuf_tensor("a_stgx", [128, 2, KD, 512], BF16) as a_stgx,
            nc.sbuf_tensor("a_stgx2", [128, 2, KD, 512], BF16) as a_stgx2,
        ):
            NTILE = 32
            lds0, mm0, evv0, evs0, sts0 = (ctr["LDS"], ctr["MM"], ctr["EVV"],
                                           ctr["EVS"], ctr["STS"])
            NSTAT = 11
            evv_p1 = lambda i: evv0 + i + 1
            evs_p1 = lambda i: evs0 + i + 1
            evv_ss = lambda o: evv0 + NTILE + o
            evs_ss = lambda o: evs0 + NTILE + o
            evv_p2 = lambda i, o: evv0 + NTILE + NSTAT + 4 * i + o
            evs_p2 = lambda i, g: evs0 + NTILE + 2 + 6 * i + g + 1
            mm_t = lambda i, g: mm0 + 8 * i + g + 1
            sts_s = lambda s2: sts0 + 32 * (s2 + 1)

            @block.sync
            def _(sy):
                for i in range(NTILE):
                    if i >= 2:
                        sy.wait_ge(sEVV, evv_p1(i - 2))
                        sy.wait_ge(sEVS, evs_p1(i - 2))
                    sy.wait_ge(sLDS, lds0 + 16 * i)
                    sy.dma_start(out=a_xb[:, i % 2], in_=x_v[:, i, :]).then_inc(sLDS, 16)
                for i in range(NTILE):
                    if i == 0:
                        sy.wait_ge(sEVV, evv0 + NTILE)
                        sy.wait_ge(sEVS, evs0 + NTILE)
                    if i >= 2:
                        sy.wait_ge(sEVV, evv_p2(i - 2, 2))
                    sy.wait_ge(sLDS, lds0 + 16 * (NTILE + i))
                    sy.dma_start(out=a_xb[:, i % 2], in_=x_v[:, i, :]).then_inc(sLDS, 16)
            ctr["LDS"] += 16 * 2 * NTILE

            @block.gpsimd
            def _(g):
                g.wait_ge(sLDG, ctr["LDG"])
                g.dma_start(out=bw0[:], in_=wqt_v[:]).then_inc(sLDG, 16)
                for s2 in range(NTILE // 4):
                    i = 4 * s2 + 3
                    g.wait_ge(sEVS, evs_p2(i, 5))
                    g.wait_ge(sEVV, evv_p2(i, 4))
                    g.wait_ge(sSTS, sts0 + 32 * s2)
                    g.dma_start(out=xnT_v[:, :, s2 * 512:(s2 + 1) * 512],
                                in_=a_stgx[:, s2 % 2]).then_inc(sSTS, 16)
                    g.wait_ge(sSTS, sts0 + 32 * s2 + 16)
                    g.dma_start(out=xn2T_v[:, :, s2 * 512:(s2 + 1) * 512],
                                in_=a_stgx2[:, s2 % 2]).then_inc(sSTS, 16)
            ctr["LDG"] += 16
            ctr["STS"] += 32 * (NTILE // 4)

            @block.scalar
            def _(sc):
                st = a_st
                for i in range(NTILE):
                    sc.wait_ge(sLDS, lds0 + 16 * (i + 1))
                    sc.activation(a_sq[:], a_xb[:, i % 2], AF.Square,
                                  accum_out=st[:, 1, i:i + 1]).then_inc(sEVS, 1)
                sc.wait_ge(sEVV, evv_ss(5))
                sc.activation(st[:, 6], st[:, 5], AF.Sqrt).then_inc(sEVS, 1)
                sc.wait_ge(sEVV, evv_ss(9))
                sc.activation(st[:, 9], st[:, 8], AF.Sqrt).then_inc(sEVS, 1)
                for i in range(NTILE):
                    for g in range(6):
                        sc.wait_ge(sMM, mm_t(i, g))
                        if i >= 1:
                            pass
                        s2 = i // 4
                        if i % 4 == 0 and s2 >= 2 and g == 0:
                            sc.wait_ge(sSTS, sts_s(s2 - 2))
                        stg = a_stgx2 if g < 4 else a_stgx
                        gg = g % 4
                        sc.activation(
                            stg[:, s2 % 2, 4 * gg:4 * gg + 4, (i % 4) * 128:(i % 4 + 1) * 128],
                            bank(g).rearrange("p (a c) -> p a c", a=4),
                            AF.Copy).then_inc(sEVS, 1)
            ctr["EVS"] += NTILE + 2 + 6 * NTILE

            @block.vector
            def _(v):
                st = a_st
                for i in range(NTILE):
                    v.wait_ge(sLDS, lds0 + 16 * (i + 1))
                    v.reduce_sum(st[:, 0, i:i + 1], a_xb[:, i % 2], axis=AX).then_inc(sEVV, 1)
                v.wait_ge(sEVV, evv0 + NTILE)
                o = 0
                def step(ins):
                    nonlocal o
                    o += 1
                    ins.then_inc(sEVV, 1)
                    v.wait_ge(sEVV, evv_ss(o))
                step(v.tensor_scalar(st[:, 2], st[:, 0], 1.0 / D, None, AluOpType.mult))
                v.wait_ge(sEVS, evs0 + NTILE)
                step(v.tensor_scalar(st[:, 3], st[:, 1], 1.0 / D, None, AluOpType.mult))
                step(v.tensor_tensor(st[:, 4], st[:, 2], st[:, 2], AluOpType.mult))
                step(v.tensor_tensor(st[:, 4], st[:, 3], st[:, 4], AluOpType.subtract))
                step(v.tensor_scalar(st[:, 5], st[:, 4], LN_EPS, None, AluOpType.add))
                v.wait_ge(sEVS, evs_ss(1))
                step(v.reciprocal(st[:, 7], st[:, 6]))
                step(v.reciprocal(st[:, 8], st[:, 5]))
                step(v.tensor_tensor(st[:, 8], st[:, 4], st[:, 8], AluOpType.mult))
                step(v.tensor_scalar(st[:, 8], st[:, 8], RMS_EPS, None, AluOpType.add))
                v.wait_ge(sEVS, evs_ss(2))
                step(v.reciprocal(st[:, 10], st[:, 9]))
                step(v.tensor_tensor(st[:, 11], st[:, 7], st[:, 10], AluOpType.mult))
                assert o == NSTAT
                for i in range(NTILE):
                    v.wait_ge(sLDS, lds0 + 16 * (NTILE + i + 1))
                    if i >= 2:
                        v.wait_ge(sMM, mm_t(i - 2, 7))
                    v.tensor_scalar(a_xn2bf[:, i % 2], a_xb[:, i % 2],
                                    st[:, 2, i:i + 1], st[:, 11, i:i + 1],
                                    AluOpType.subtract, AluOpType.mult).then_inc(sEVV, 1)
                    v.tensor_scalar(a_xnbf[:, i % 2], a_xb[:, i % 2],
                                    st[:, 2, i:i + 1], st[:, 7, i:i + 1],
                                    AluOpType.subtract, AluOpType.mult).then_inc(sEVV, 1)
                    s2 = i // 4
                    for g in (6, 7):
                        v.wait_ge(sMM, mm_t(i, g))
                        if i % 4 == 0 and s2 >= 2 and g == 6:
                            v.wait_ge(sSTS, sts_s(s2 - 2))
                        stg = a_stgx
                        gg = g % 4
                        v.tensor_copy(
                            stg[:, s2 % 2, 4 * gg:4 * gg + 4, (i % 4) * 128:(i % 4 + 1) * 128],
                            bank(g).rearrange("p (a c) -> p a c", a=4)).then_inc(sEVV, 1)
            ctr["EVV"] += NTILE + NSTAT + 4 * NTILE

            @block.tensor
            def _(t):
                for i in range(NTILE):
                    t.wait_ge(sEVV, evv_p2(i, 2))
                    for g in range(8):
                        if i >= 1:
                            if g < 6:
                                t.wait_ge(sEVS, evs_p2(i - 1, g))
                            else:
                                t.wait_ge(sEVV, evv_p2(i - 1, 2 + (g - 6) + 1))
                        src = a_xn2bf if g < 4 else a_xnbf
                        ins = None
                        for kk in range(4):
                            k = (g % 4) * 4 + kk
                            ins = t.matmul(region(g, kk * 128, 128),
                                           src[:, i % 2, k * 128:(k + 1) * 128],
                                           ident[:], start=True, stop=True)
                        ins.then_inc(sMM, 1)
            ctr["MM"] += 8 * NTILE

        barrier()

        if upto_i >= 1:
            with (
                nc.sbuf_tensor("b_w1", [128, KD, D], BF16) as b_w1,
                nc.sbuf_tensor("b_x", [128, 2, KD, 512], BF16) as b_x,
                nc.sbuf_tensor("b_qstg", [128, 2, 512], BF16) as b_qstg,
                nc.sbuf_tensor("b_kstg", [128, 2, D], BF16) as b_kstg,
            ):
                b_bufs = [bw0, b_w1, bw0]
                lds0, ldg0, mm0, evs0, sts0 = (ctr["LDS"], ctr["LDG"], ctr["MM"],
                                               ctr["EVS"], ctr["STS"])
                mm_g = lambda p, ts, g: mm0 + 128 * p + 16 * ts + g + 1
                evs_g = lambda p, ts, g: evs0 + 128 * p + 16 * ts + g + 1

                sts_sched = []
                stv = sts0
                store_of_stream = {}
                for p in range(3):
                    for ts in range(8):
                        if p == 0:
                            for f in range(16):
                                stv += 16
                                store_of_stream[("q", ts, f)] = stv
                        else:
                            for ib in range(4):
                                stv += 16
                                store_of_stream[("kv", p, ts, ib)] = stv

                @block.gpsimd
                def _(g):
                    g.wait_ge(sLDG, ldg0)
                    g.dma_start(out=b_w1[:], in_=wkt_v[:]).then_inc(sLDG, 16)
                    g.wait_ge(sMM, mm_g(0, 7, 15))
                    g.wait_ge(sLDG, ldg0 + 16)
                    g.dma_start(out=bw0[:], in_=wvt_v[:]).then_inc(sLDG, 16)
                ctr["LDG"] += 32

                @block.sync
                def _(sy):
                    for p in range(3):
                        for ts in range(8):
                            idx = 8 * p + ts
                            if idx >= 2:
                                sy.wait_ge(sMM, mm0 + 16 * (idx - 1))
                            sy.wait_ge(sLDS, lds0 + 16 * idx)
                            sy.dma_start(out=b_x[:, idx % 2],
                                         in_=xn2T_v[:, :, ts * 512:(ts + 1) * 512]
                                         ).then_inc(sLDS, 16)
                ctr["LDS"] += 16 * 24

                @block.tensor
                def _(t):
                    for p in range(3):
                        wbuf = b_bufs[p][:]
                        if p > 0:
                            t.wait_ge(sLDG, ldg0 + 16 * p)
                        for ts in range(8):
                            idx = 8 * p + ts
                            t.wait_ge(sLDS, lds0 + 16 * (idx + 1))
                            xbuf = b_x[:, idx % 2]
                            for g in range(16):
                                gi = 128 * p + 16 * ts + g
                                if gi >= 4:
                                    pi, r = divmod(gi - 4, 128)
                                    ti, g2 = divmod(r, 16)
                                    t.wait_ge(sEVS, evs_g(pi, ti, g2))
                                ins = None
                                if p == 0:
                                    for k in range(KD):
                                        ins = t.matmul(bank(g % 4),
                                                       wbuf[:, k, g * 128:(g + 1) * 128],
                                                       xbuf[:, k, :],
                                                       start=(k == 0), stop=(k == KD - 1))
                                else:
                                    ib, ot = divmod(g, 4)
                                    for k in range(KD):
                                        ins = t.matmul(bank(g % 4),
                                                       xbuf[:, k, ib * 128:(ib + 1) * 128],
                                                       wbuf[:, k, ot * 512:(ot + 1) * 512],
                                                       start=(k == 0), stop=(k == KD - 1))
                                ins.then_inc(sMM, 1)
                ctr["MM"] += 3 * 128

                @block.scalar
                def _(sc):
                    for p in range(3):
                        for ts in range(8):
                            for g in range(16):
                                sc.wait_ge(sMM, mm_g(p, ts, g))
                                if p == 0:
                                    sf = 16 * ts + g
                                    if sf >= 2:
                                        sc.wait_ge(sSTS, store_of_stream[("q", (sf - 2) // 16,
                                                                          (sf - 2) % 16)])
                                    sc.activation(b_qstg[:, sf % 2], bank(g % 4),
                                                  AF.Silu).then_inc(sEVS, 1)
                                    sc.wait_ge(sEVS, evs_g(p, ts, g))
                                    sc.wait_ge(sSTS, store_of_stream[("q", ts, g)] - 16)
                                    sc.dma_start(out=qT_v[:, g, ts * 512:(ts + 1) * 512],
                                                 in_=b_qstg[:, sf % 2]).then_inc(sSTS, 16)
                                else:
                                    ib, ot = divmod(g, 4)
                                    ks = (p - 1) * 32 + ts * 4 + ib
                                    if ot == 0 and ks >= 2:
                                        k2 = ks - 2
                                        p2 = k2 // 32 + 1
                                        t2, i2 = divmod(k2 % 32, 4)
                                        sc.wait_ge(sSTS, store_of_stream[("kv", p2, t2, i2)])
                                    sc.activation(b_kstg[:, ks % 2, ot * 512:(ot + 1) * 512],
                                                  bank(g % 4), AF.Silu).then_inc(sEVS, 1)
                                    if ot == 3:
                                        sc.wait_ge(sEVS, evs_g(p, ts, g))
                                        sc.wait_ge(sSTS, store_of_stream[("kv", p, ts, ib)] - 16)
                                        dst = k_v if p == 1 else v_v
                                        sc.dma_start(out=dst[:, ts * 4 + ib, :],
                                                     in_=b_kstg[:, ks % 2]).then_inc(sSTS, 16)
                ctr["EVS"] += 3 * 128
                ctr["STS"] = stv

            barrier()
        _ab_es.close()

        if upto_i >= 2:
            with (
                nc.sbuf_tensor("c_kv", [128, KD, D], F32) as c_kv,
                nc.sbuf_tensor("c_q", [128, 2, KD, CH], BF16) as c_q,
                nc.sbuf_tensor("c_k", [128, 2, 2, D], BF16) as c_k,
                nc.sbuf_tensor("c_v", [128, 2, D], BF16) as c_v,
                nc.sbuf_tensor("c_kT", [128, KD, CH], BF16) as c_kT,
                nc.sbuf_tensor("c_sT", [128, 2, CH], BF16) as c_sT,
                nc.sbuf_tensor("c_oT", [128, 2, KD, CH], BF16) as c_oT,
                nc.sbuf_tensor("c_kvbf", [128, 4, 1024], BF16) as c_kvbf,
                nc.sbuf_tensor("masks", [128, 512], F32) as masks,
            ):
                lds0, mm0, evv0, evs0, sts0 = (ctr["LDS"], ctr["MM"], ctr["EVV"],
                                               ctr["EVS"], ctr["STS"])
                def mm_base(t):
                    return mm0 + (0 if t == 0 else 42 + 74 * (t - 1))

                def mm_end(t):
                    return mm0 + 42 + 74 * t if t >= 0 else mm0

                def mm_a(t, g):
                    return mm_base(t) + g + 1

                def mm_b(t, j):
                    return mm_base(t) + 8 + j + 1

                def mm_ik(t, h, k):
                    return mm_base(t) + 10 + 24 * h + k + 1

                def mm_ie(t, h, e):
                    if t == 0:
                        return mm_base(t) + 10 + 8 * h + e + 1
                    return mm_base(t) + 10 + 24 * h + 16 + e + 1

                def mm_d(t, k):
                    if t == 0:
                        return mm_base(t) + 26 + k + 1
                    return mm_base(t) + 58 + k + 1

                evv_c = lambda t, o: evv0 + 42 * t + o
                evs_stage = lambda t, h, k: evs0 + 32 * (t - 1) + 16 * h + k + 1
                ldk0, ldv0 = ctr["LDK"], ctr["LDV"]
                sts_o = lambda t: sts0 + 16 * (t + 1)

                @block.sync
                def _(sy):
                    for t in range(NCH):
                        if t >= 2:
                            sy.wait_ge(sMM, mm_end(t - 2))
                        sy.wait_ge(sLDS, lds0 + 16 * t)
                        sy.dma_start(out=c_q[:, t % 2],
                                     in_=qT_v[:, :, t * CH:(t + 1) * CH]).then_inc(sLDS, 16)
                        sy.wait_ge(sLDK, ldk0 + 16 * t)
                        sy.dma_start(out=c_k[:, t % 2],
                                     in_=k_v[:, 2 * t:2 * t + 2, :]).then_inc(sLDK, 16)
                        if t >= 1:
                            sy.wait_ge(sMM, mm_end(t - 1))
                        sy.wait_ge(sLDV, ldv0 + 16 * t)
                        sy.dma_start(out=c_v[:],
                                     in_=v_v[:, 2 * t:2 * t + 2, :]).then_inc(sLDV, 16)
                        if t >= 1:
                            sy.wait_ge(sEVV, evv_c(t - 1, 26))
                            sy.wait_ge(sSTS, sts_o(t - 1) - 16)
                            sy.dma_start(out=oT_v[:, :, (t - 1) * CH:t * CH],
                                         in_=c_oT[:, (t - 1) % 2]).then_inc(sSTS, 16)
                    sy.wait_ge(sEVV, evv_c(NCH - 1, 26))
                    sy.wait_ge(sSTS, sts_o(NCH - 1) - 16)
                    sy.dma_start(out=oT_v[:, :, (NCH - 1) * CH:NCH * CH],
                                 in_=c_oT[:, (NCH - 1) % 2]).then_inc(sSTS, 16)
                ctr["LDS"] += 16 * NCH
                ctr["LDK"] += 16 * NCH
                ctr["LDV"] += 16 * NCH
                ctr["STS"] += 16 * NCH

                ldgC0 = ctr["LDG"]

                @block.gpsimd
                def _(g):
                    g.wait_ge(sLDG, ldgC0)
                    g.dma_start(out=masks[:], in_=masks_in[:]).then_inc(sLDG, 16)
                ctr["LDG"] += 16

                @block.scalar
                def _(sc):
                    for t in range(1, NCH):
                        for h in range(2):
                            for k in range(KD):
                                idx = 16 * h + k
                                sc.wait_ge(sEVV, evv_c(t - 1, 26 + k + 1))
                                if idx >= 4:
                                    h2, k2 = divmod(idx - 4, 16)
                                    sc.wait_ge(sMM, mm_ik(t, h2, k2))
                                elif t >= 2:
                                    h2, k2 = divmod(idx - 4 + 32, 16)
                                    sc.wait_ge(sMM, mm_ik(t - 1, h2, k2))
                                sc.activation(c_kvbf[:, idx % 4],
                                              c_kv[:, k, h * 1024:(h + 1) * 1024],
                                              AF.Copy).then_inc(sEVS, 1)
                ctr["EVS"] += 32 * (NCH - 1)

                @block.tensor
                def _(t_):
                    for t in range(NCH):
                        t_.wait_ge(sLDS, lds0 + 16 * (t + 1))
                        t_.wait_ge(sLDK, ldk0 + 16 * (t + 1))
                        for g in range(8):
                            if t >= 1:
                                t_.wait_ge(sEVV, evv_c(t - 1, 41 if g < 4 else 42))
                            j, ks = divmod(4 * g, 16)
                            ins = None
                            for kk in range(4):
                                k = ks + kk
                                ins = t_.matmul(region(g, kk * 128, 128),
                                                c_k[:, t % 2, j, k * 128:(k + 1) * 128],
                                                ident[:], start=True, stop=True)
                            ins.then_inc(sMM, 1)
                        for j in range(2):
                            t_.wait_ge(sEVV, evv_c(t, 4 * (j + 1)))
                            ins = None
                            for k in range(KD):
                                ins = t_.matmul(region(j, 0, CH),
                                                c_kT[:, k, j * 128:(j + 1) * 128],
                                                c_q[:, t % 2, k, :],
                                                start=(k == 0), stop=(k == KD - 1))
                            ins.then_inc(sMM, 1)
                        t_.wait_ge(sEVV, evv_c(t, 10))
                        t_.wait_ge(sLDV, ldv0 + 16 * (t + 1))
                        for h in range(2):
                            if h == 1:
                                t_.wait_ge(sEVV, evv_c(t, 18))
                            if t >= 1:
                                for k in range(KD):
                                    idx = 16 * h + k
                                    t_.wait_ge(sEVS, evs_stage(t, h, k))
                                    for b in range(8):
                                        ins = t_.matmul(
                                            region(b, 0, CH),
                                            c_kvbf[:, idx % 4, b * 128:(b + 1) * 128],
                                            c_q[:, t % 2, k, :],
                                            start=(k == 0), stop=False,
                                            skip_group_check=True)
                                    ins.then_inc(sMM, 1)
                            for b in range(8):
                                e = 8 * h + b
                                ins = None
                                for j in range(2):
                                    ins = t_.matmul(region(b, 0, CH),
                                                    c_v[:, j, e * 128:(e + 1) * 128],
                                                    c_sT[:, j, :],
                                                    start=(t == 0 and j == 0),
                                                    stop=(j == 1),
                                                    skip_group_check=True)
                                ins.then_inc(sMM, 1)
                        t_.wait_ge(sEVV, evv_c(t, 26))
                        for k in range(KD):
                            gk = k % 2
                            if k >= 2:
                                t_.wait_ge(sEVV, evv_c(t, 26 + k - 1))
                            ins = None
                            for et in range(4):
                                for j in range(2):
                                    ins = t_.matmul(bank(gk * 4 + et),
                                                    c_k[:, t % 2, j, k * 128:(k + 1) * 128],
                                                    c_v[:, j, et * 512:(et + 1) * 512],
                                                    start=(j == 0), stop=(j == 1),
                                                    skip_group_check=True)
                            ins.then_inc(sMM, 1)
                ctr["MM"] += 42 + 74 * (NCH - 1)

                @block.vector
                def _(v):
                    v.wait_ge(sLDG, ldgC0 + 16)
                    for t in range(NCH):
                        for g in range(8):
                            v.wait_ge(sMM, mm_a(t, g))
                            j, ks = divmod(4 * g, 16)
                            v.tensor_copy(c_kT[:, ks:ks + 4, j * 128:(j + 1) * 128],
                                          bank(g).rearrange("p (a c) -> p a c", a=4)
                                          ).then_inc(sEVV, 1)
                        for j in range(2):
                            v.wait_ge(sMM, mm_b(t, j))
                            v.tensor_tensor(c_sT[:, j, :], region(j, 0, CH),
                                            masks[:, j * 256:(j + 1) * 256],
                                            AluOpType.mult).then_inc(sEVV, 1)
                        if t >= 2:
                            v.wait_ge(sSTS, sts_o(t - 2))
                        for h in range(2):
                            for b in range(8):
                                e = 8 * h + b
                                v.wait_ge(sMM, mm_ie(t, h, b))
                                v.tensor_copy(c_oT[:, t % 2, e, :],
                                              region(b, 0, CH)).then_inc(sEVV, 1)
                        for k in range(KD):
                            gk = k % 2
                            v.wait_ge(sMM, mm_d(t, k))
                            if t >= 1:
                                v.wait_ge(sEVS, evs_stage(t, 1, k))
                            span = psum[:, gk * 2048:(gk + 1) * 2048]
                            dst = c_kv[:, k, :]
                            if t == 0:
                                ins = v.tensor_copy(dst, span)
                            else:
                                ins = v.tensor_tensor(dst, span, dst, AluOpType.add)
                            ins.then_inc(sEVV, 1)
                ctr["EVV"] += 42 * NCH

            barrier()

        if upto_i >= 3:
            with (
                nc.sbuf_tensor("d_wg", [128, KD, D], BF16) as d_wg,
                nc.sbuf_tensor("d_wp", [128, KD, D], BF16) as d_wp,
                nc.sbuf_tensor("d_oT", [128, 2, KD, 256], BF16) as d_oT,
                nc.sbuf_tensor("d_x2", [128, 2, KD, 256], BF16) as d_x2,
                nc.sbuf_tensor("d_og", [128, 2, KD, 256], BF16) as d_og,
                nc.sbuf_tensor("d_sig", [128, 2, 256], F32) as d_sig,
                nc.sbuf_tensor("d_sstg", [128, 2, D], F32) as d_sstg,
            ):
                lds0, ldg0, mm0, evv0, evs0, sts0 = (ctr["LDS"], ctr["LDG"], ctr["MM"],
                                                     ctr["EVV"], ctr["EVS"], ctr["STS"])
                NSL = 16
                mm_pg = lambda t, g: mm0 + 24 * t + g + 1
                mm_ps = lambda t, s: mm0 + 24 * t + 16 + s + 1
                evs_sg = lambda t, g: evs0 + 16 * t + g + 1
                evv_og = lambda t, g: evv0 + 24 * t + g + 1
                evv_ev = lambda t, s: evv0 + 24 * t + 16 + s + 1
                sts_d = lambda t, i: sts0 + 32 * t + 16 * (i + 1)

                @block.gpsimd
                def _(g):
                    g.wait_ge(sLDG, ldg0)
                    g.dma_start(out=d_wg[:, :, 0:1024], in_=wgt_v[:, :, 0:1024]).then_inc(sLDG, 16)
                    g.wait_ge(sLDG, ldg0 + 16)
                    g.dma_start(out=d_wg[:, :, 1024:2048], in_=wgt_v[:, :, 1024:2048]).then_inc(sLDG, 16)
                    g.wait_ge(sLDG, ldg0 + 32)
                    g.dma_start(out=d_wp[:], in_=wpt_v[:]).then_inc(sLDG, 16)
                ctr["LDG"] += 48

                @block.sync
                def _(sy):
                    for t in range(NSL):
                        if t >= 2:
                            sy.wait_ge(sEVV, evv_og(t - 2, 15))
                            sy.wait_ge(sMM, mm_pg(t - 2, 15))
                        sy.wait_ge(sLDS, lds0 + 32 * t)
                        sy.dma_start(out=d_oT[:, t % 2],
                                     in_=oT_v[:, :, t * 256:(t + 1) * 256]).then_inc(sLDS, 16)
                        sy.wait_ge(sLDS, lds0 + 32 * t + 16)
                        sy.dma_start(out=d_x2[:, t % 2],
                                     in_=xn2T_v[:, :, t * 256:(t + 1) * 256]).then_inc(sLDS, 16)
                        if t >= 1:
                            for i in range(2):
                                sy.wait_ge(sEVV, evv_ev(t - 1, 4 * i + 3))
                                sy.wait_ge(sSTS, sts_d(t - 1, i) - 16)
                                sy.dma_start(out=sout_v[:, (t - 1) * 2 + i, :],
                                             in_=d_sstg[:, i]).then_inc(sSTS, 16)
                    for i in range(2):
                        sy.wait_ge(sEVV, evv_ev(NSL - 1, 4 * i + 3))
                        sy.wait_ge(sSTS, sts_d(NSL - 1, i) - 16)
                        sy.dma_start(out=sout_v[:, (NSL - 1) * 2 + i, :],
                                     in_=d_sstg[:, i]).then_inc(sSTS, 16)
                ctr["LDS"] += 32 * NSL
                ctr["STS"] += 32 * NSL

                @block.tensor
                def _(t_):
                    t_.wait_ge(sLDG, ldg0 + 16)
                    for t in range(NSL):
                        t_.wait_ge(sLDS, lds0 + 32 * t + 32)
                        if t == 0:
                            pass
                        for g in range(16):
                            if t == 0 and g == 8:
                                t_.wait_ge(sLDG, ldg0 + 32)
                            b = g % 8
                            if g < 8:
                                if t >= 1:
                                    t_.wait_ge(sEVV, evv_ev(t - 1, b + 1))
                            else:
                                t_.wait_ge(sEVS, evs_sg(t, g - 8))
                            ins = None
                            for k in range(KD):
                                ins = t_.matmul(region(b, 0, 256),
                                                d_wg[:, k, g * 128:(g + 1) * 128],
                                                d_x2[:, t % 2, k, :],
                                                start=(k == 0), stop=(k == KD - 1))
                            ins.then_inc(sMM, 1)
                        if t == 0:
                            t_.wait_ge(sLDG, ldg0 + 48)
                        t_.wait_ge(sEVV, evv_og(t, 15))
                        for s in range(8):
                            i, ot = divmod(s, 4)
                            b = s % 8
                            t_.wait_ge(sEVS, evs_sg(t, 8 + b))
                            ins = None
                            for e in range(KD):
                                ins = t_.matmul(bank(b),
                                                d_og[:, t % 2, e, i * 128:(i + 1) * 128],
                                                d_wp[:, e, ot * 512:(ot + 1) * 512],
                                                start=(e == 0), stop=(e == KD - 1))
                            ins.then_inc(sMM, 1)
                ctr["MM"] += 24 * NSL

                @block.scalar
                def _(sc):
                    for t in range(NSL):
                        for g in range(16):
                            sc.wait_ge(sMM, mm_pg(t, g))
                            if g >= 2 or t >= 1:
                                g2 = g - 2 if g >= 2 else g + 14
                                t2 = t if g >= 2 else t - 1
                                sc.wait_ge(sEVV, evv_og(t2, g2))
                            sc.activation(d_sig[:, g % 2], region(g % 8, 0, 256),
                                          AF.Sigmoid).then_inc(sEVS, 1)
                ctr["EVS"] += 16 * NSL

                @block.vector
                def _(v):
                    for t in range(NSL):
                        if t >= 2:
                            v.wait_ge(sMM, mm_ps(t - 2, 8))
                        for g in range(16):
                            v.wait_ge(sEVS, evs_sg(t, g))
                            v.tensor_tensor(d_og[:, t % 2, g, :], d_oT[:, t % 2, g, :],
                                            d_sig[:, g % 2], AluOpType.mult).then_inc(sEVV, 1)
                        for s in range(8):
                            i, ot = divmod(s, 4)
                            v.wait_ge(sMM, mm_ps(t, s))
                            if t >= 1 and s % 4 == 0:
                                v.wait_ge(sSTS, sts_d(t - 1, i))
                            v.tensor_copy(d_sstg[:, i, ot * 512:(ot + 1) * 512],
                                          bank(s % 8)).then_inc(sEVV, 1)
                ctr["EVV"] += 24 * NSL

            barrier()

        if upto_i >= 4:
            with (
                nc.sbuf_tensor("e_wg", [128, 2, KD, 1024], BF16) as e_wg,
                nc.sbuf_tensor("e_wu", [128, 2, KD, 1024], BF16) as e_wu,
                nc.sbuf_tensor("e_x", [128, 2, KD, 512], BF16) as e_x,
                nc.sbuf_tensor("e_tg", [128, 2, 512], F32) as e_tg,
                nc.sbuf_tensor("e_tg2", [128, 2, 512], F32) as e_tg2,
                nc.sbuf_tensor("e_tu", [128, 2, 512], F32) as e_tu,
                nc.sbuf_tensor("e_fstg", [128, 4, 512], BF16) as e_fstg,
                nc.sbuf_tensor("sg_sb", [128, 64], F32) as sg_sb,
                nc.sbuf_tensor("su_sb", [128, 64], F32) as su_sb,
            ):
                lds0, ldg0, mm0, evv0, evs0, sts0 = (ctr["LDS"], ctr["LDG"], ctr["MM"],
                                                     ctr["EVV"], ctr["EVS"], ctr["STS"])
                NP = 8
                mm_e = lambda idx, g: mm0 + 16 * idx + g + 1
                evv_e = lambda idx, f, j: evv0 + 24 * idx + 3 * f + j + 1
                evs_e = lambda idx, f: evs0 + 8 * idx + f + 1
                sts_e = lambda sf: sts0 + 32 * (sf + 1)
                ldg_e = lambda pc: ldg0 + 32 + 32 * (pc + 1)

                @block.gpsimd
                def _(g):
                    g.wait_ge(sLDG, ldg0)
                    g.dma_start(out=sg_sb[:], in_=s_gate[:]).then_inc(sLDG, 16)
                    g.wait_ge(sLDG, ldg0 + 16)
                    g.dma_start(out=su_sb[:], in_=s_up[:]).then_inc(sLDG, 16)
                    for pc in range(NP):
                        if pc >= 2:
                            g.wait_ge(sMM, mm0 + 16 * 8 * (pc - 1))
                        g.wait_ge(sLDG, ldg0 + 32 + 32 * pc)
                        g.dma_start(out=e_wg[:, pc % 2],
                                    in_=wgate_v[:, :, pc * 1024:(pc + 1) * 1024]
                                    ).then_inc(sLDG, 16)
                        g.wait_ge(sLDG, ldg0 + 32 + 32 * pc + 16)
                        g.dma_start(out=e_wu[:, pc % 2],
                                    in_=wup_v[:, :, pc * 1024:(pc + 1) * 1024]
                                    ).then_inc(sLDG, 16)
                ctr["LDG"] += 32 + 32 * NP

                @block.sync
                def _(sy):
                    for pc in range(NP):
                        for ts in range(8):
                            idx = 8 * pc + ts
                            if idx >= 2:
                                sy.wait_ge(sMM, mm0 + 16 * (idx - 1))
                            sy.wait_ge(sLDS, lds0 + 16 * idx)
                            sy.dma_start(out=e_x[:, idx % 2],
                                         in_=xnT_v[:, :, ts * 512:(ts + 1) * 512]
                                         ).then_inc(sLDS, 16)
                ctr["LDS"] += 16 * 64
                ctr["STS"] += 32 * 512

                @block.tensor
                def _(t_):
                    for pc in range(NP):
                        t_.wait_ge(sLDG, ldg_e(pc))
                        for ts in range(8):
                            idx = 8 * pc + ts
                            t_.wait_ge(sLDS, lds0 + 16 * (idx + 1))
                            for g in range(16):
                                f = g // 2
                                gi = 16 * idx + g
                                if gi >= 4:
                                    i2, g2 = divmod(gi - 4, 16)
                                    f2, is_up = divmod(g2, 2)
                                    t_.wait_ge(sEVV, evv_e(i2, f2, 1 + is_up))
                                wsrc = e_wg if g % 2 == 0 else e_wu
                                ins = None
                                for k in range(KD):
                                    ins = t_.matmul(bank(g % 4),
                                                    wsrc[:, pc % 2, k, f * 128:(f + 1) * 128],
                                                    e_x[:, idx % 2, k, :],
                                                    start=(k == 0), stop=(k == KD - 1))
                                ins.then_inc(sMM, 1)
                ctr["MM"] += 16 * 64

                @block.scalar
                def _(sc):
                    def fstore(sf):
                        i2, f2 = divmod(sf, 8)
                        pc2, ts2 = divmod(i2, 8)
                        kf = pc2 * 8 + f2
                        sc.wait_ge(sEVV, evv_e(i2, f2, 2))
                        sc.wait_ge(sSTS, sts_e(sf) - 32)
                        sc.dma_start(out=fuse_d[2 * ts2, :, kf, :],
                                     in_=e_fstg[:, sf % 4, 0:256]).then_inc(sSTS, 16)
                        sc.wait_ge(sSTS, sts_e(sf) - 16)
                        sc.dma_start(out=fuse_d[2 * ts2 + 1, :, kf, :],
                                     in_=e_fstg[:, sf % 4, 256:512]).then_inc(sSTS, 16)
                    for idx in range(64):
                        for f in range(8):
                            sc.wait_ge(sEVV, evv_e(idx, f, 1))
                            sf = 8 * idx + f
                            if sf >= 2:
                                i2, f2 = divmod(sf - 2, 8)
                                sc.wait_ge(sEVV, evv_e(i2, f2, 2))
                            sc.activation(e_tg2[:, f % 2], e_tg[:, f % 2],
                                          AF.Silu).then_inc(sEVS, 1)
                            if sf >= 2:
                                fstore(sf - 2)
                    fstore(510)
                    fstore(511)
                ctr["EVS"] += 8 * 64

                @block.vector
                def _(v):
                    for idx in range(64):
                        pc = idx // 8
                        for f in range(8):
                            sf = 8 * idx + f
                            c = pc * 8 + f
                            v.wait_ge(sMM, mm_e(idx, 2 * f))
                            if sf >= 2:
                                i2, f2 = divmod(sf - 2, 8)
                                v.wait_ge(sEVS, evs_e(i2, f2))
                            v.tensor_scalar(e_tg[:, f % 2], bank((2 * f) % 4),
                                            sg_sb[:, c:c + 1], None,
                                            AluOpType.mult).then_inc(sEVV, 1)
                            v.wait_ge(sMM, mm_e(idx, 2 * f + 1))
                            v.tensor_scalar(e_tu[:, f % 2], bank((2 * f + 1) % 4),
                                            su_sb[:, c:c + 1], None,
                                            AluOpType.mult).then_inc(sEVV, 1)
                            v.wait_ge(sEVV, evv_e(idx, f, 1))
                            v.wait_ge(sEVS, evs_e(idx, f))
                            if sf >= 4:
                                v.wait_ge(sSTS, sts_e(sf - 4))
                            v.tensor_tensor(e_fstg[:, sf % 4], e_tg2[:, f % 2],
                                            e_tu[:, f % 2], AluOpType.mult).then_inc(sEVV, 1)
                ctr["EVV"] += 24 * 64

            barrier()

        if upto_i >= 5:
            with (
                nc.sbuf_tensor("f_wd", [128, 64, 512], BF16) as f_wd,
                nc.sbuf_tensor("f_fu", [128, 2, 64, 256], BF16) as f_fu,
                nc.sbuf_tensor("f_sdq", [128, 512], F32) as f_sdq,
                nc.sbuf_tensor("sd_row", [1, D], F32) as sd_row,
                nc.sbuf_tensor("ones_sb", [1, 128], F32) as ones_sb,
                nc.sbuf_tensor("f_so", [128, 2, 512], F32) as f_so,
                nc.sbuf_tensor("f_x", [128, 2, 512], F32) as f_x,
                nc.sbuf_tensor("f_t1", [128, 2, 512], F32) as f_t1,
                nc.sbuf_tensor("f_t2", [128, 2, 512], F32) as f_t2,
                nc.sbuf_tensor("f_os", [128, 2, 512], F32) as f_os,
            ):
                lds0, ldg0, mm0, evv0, evs0, sts0 = (ctr["LDS"], ctr["LDG"], ctr["MM"],
                                                     ctr["EVV"], ctr["EVS"], ctr["STS"])
                def mm_q(q):
                    return mm0 + 33 * q + 1

                def mm_g(q, gg):
                    return mm0 + 33 * q + 1 + gg + 1

                def evv_q(q):
                    return evv0 + 97 * q + 1

                def evv_g(q, gg, j):
                    return evv0 + 97 * q + 1 + 3 * gg + j

                evs_g = lambda q, gg: evs0 + 32 * q + gg + 1
                ldf0 = ctr["LDF"]
                lds_fu = lambda q, blk: ldf0 + 16 * (16 * q + blk + 1)
                lds_sx = lambda q, blk, i: lds0 + 1024 * q + 64 * blk + 32 * (i + 1)
                sts_g = lambda q, gg: sts0 + 16 * (32 * q + gg + 1)

                @block.gpsimd
                def _(g):
                    g.wait_ge(sLDG, ldg0)
                    g.dma_start(out=sd_row[:], in_=s_down[None, :]).then_inc(sLDG, 16)
                    g.wait_ge(sLDG, ldg0 + 16)
                    g.dma_start(out=ones_sb[:], in_=ones_in[:]).then_inc(sLDG, 16)
                    for q in range(4):
                        if q >= 1:
                            g.wait_ge(sMM, mm_g(q - 1, 31))
                        g.wait_ge(sLDG, ldg0 + 32 + 16 * q)
                        g.dma_start(out=f_wd[:],
                                    in_=wdown_v[:, :, q * 512:(q + 1) * 512]
                                    ).then_inc(sLDG, 16)
                ctr["LDG"] += 96

                @block.sync
                def _(sy):
                    def fuse_load(fidx):
                        q3, blk3 = divmod(fidx, 16)
                        if fidx >= 2:
                            pgi = 2 * fidx - 3
                            q2, g2 = divmod(pgi, 32)
                            sy.wait_ge(sMM, mm_g(q2, g2))
                        sy.wait_ge(sLDF, ldf0 + 16 * fidx)
                        sy.dma_start(out=f_fu[:, fidx % 2],
                                     in_=fuse_d[blk3]).then_inc(sLDF, 16)

                    def out_store(gi):
                        q2, g2 = divmod(gi, 32)
                        sy.wait_ge(sEVV, evv_g(q2, g2, 3))
                        sy.wait_ge(sSTS, sts_g(q2, g2) - 16)
                        blk2, i2 = divmod(g2, 2)
                        sy.dma_start(out=out_v[:, 2 * blk2 + i2, q2 * 512:(q2 + 1) * 512],
                                     in_=f_os[:, gi % 2]).then_inc(sSTS, 16)

                    fuse_load(0)
                    for q in range(4):
                        for blk in range(16):
                            fidx = 16 * q + blk
                            if fidx + 1 < 64:
                                fuse_load(fidx + 1)
                            for i in range(2):
                                gi = 32 * q + 2 * blk + i
                                if gi >= 2:
                                    q2, g2 = divmod(gi - 2, 32)
                                    sy.wait_ge(sEVV, evv_g(q2, g2, 3))
                                sy.wait_ge(sLDS, lds_sx(q, blk, i) - 32)
                                sy.dma_start(out=f_so[:, gi % 2],
                                             in_=sout_v[:, 2 * blk + i, q * 512:(q + 1) * 512]
                                             ).then_inc(sLDS, 16)
                                sy.wait_ge(sLDS, lds_sx(q, blk, i) - 16)
                                sy.dma_start(out=f_x[:, gi % 2],
                                             in_=x_v[:, 2 * blk + i, q * 512:(q + 1) * 512]
                                             ).then_inc(sLDS, 16)
                                if gi >= 2:
                                    out_store(gi - 2)
                    out_store(126)
                    out_store(127)
                    sy.wait_ge(sSTS, sts0 + 16 * 128)
                ctr["LDS"] += 4 * 16 * 64
                ctr["LDF"] += 16 * 64
                ctr["STS"] += 16 * 128

                @block.tensor
                def _(t_):
                    t_.wait_ge(sLDG, ldg0 + 32)
                    for q in range(4):
                        t_.wait_ge(sLDG, ldg0 + 32 + 16 * (q + 1))
                        if q >= 1:
                            t_.wait_ge(sEVV, evv_q(q - 1) + 96)
                        t_.matmul(bank(4), ones_sb[:], sd_row[:, q * 512:(q + 1) * 512],
                                  start=True, stop=True).then_inc(sMM, 1)
                        for blk in range(16):
                            t_.wait_ge(sLDF, lds_fu(q, blk))
                            for i in range(2):
                                gg = 2 * blk + i
                                gi = 32 * q + gg
                                if gi >= 4:
                                    q2, g2 = divmod(gi - 4, 32)
                                    t_.wait_ge(sEVV, evv_g(q2, g2, 1))
                                ins = None
                                for k in range(64):
                                    ins = t_.matmul(bank(gg % 4),
                                                    f_fu[:, blk % 2, k, i * 128:(i + 1) * 128],
                                                    f_wd[:, k, :],
                                                    start=(k == 0), stop=(k == 63))
                                ins.then_inc(sMM, 1)
                ctr["MM"] += 33 * 4

                @block.scalar
                def _(sc):
                    for q in range(4):
                        for gg in range(32):
                            sc.wait_ge(sEVV, evv_g(q, gg, 1))
                            gi = 32 * q + gg
                            if gi >= 2:
                                q2, g2 = divmod(gi - 2, 32)
                                sc.wait_ge(sEVV, evv_g(q2, g2, 2))
                            sc.activation(f_t2[:, gg % 2], f_t1[:, gg % 2],
                                          AF.Silu).then_inc(sEVS, 1)
                ctr["EVS"] += 32 * 4

                @block.vector
                def _(v):
                    for q in range(4):
                        v.wait_ge(sMM, mm_q(q))
                        v.tensor_copy(f_sdq[:], bank(4)).then_inc(sEVV, 1)
                        v.wait_ge(sEVV, evv_q(q))
                        for blk in range(16):
                            for i in range(2):
                                gg = 2 * blk + i
                                gi = 32 * q + gg
                                v.wait_ge(sMM, mm_g(q, gg))
                                if gi >= 2:
                                    q2, g2 = divmod(gi - 2, 32)
                                    v.wait_ge(sEVS, evs_g(q2, g2))
                                v.tensor_tensor(f_t1[:, gg % 2], bank(gg % 4), f_sdq[:],
                                                AluOpType.mult).then_inc(sEVV, 1)
                                v.wait_ge(sEVS, evs_g(q, gg))
                                v.wait_ge(sLDS, lds_sx(q, blk, i))
                                v.tensor_tensor(f_t1[:, gg % 2], f_t2[:, gg % 2],
                                                f_so[:, gi % 2], AluOpType.mult).then_inc(sEVV, 1)
                                v.wait_ge(sEVV, evv_g(q, gg, 2))
                                if gi >= 2:
                                    v.wait_ge(sSTS, sts_g(*divmod(gi - 2, 32)))
                                v.tensor_tensor(f_os[:, gi % 2], f_t1[:, gg % 2],
                                                f_x[:, gi % 2], AluOpType.add).then_inc(sEVV, 1)
                ctr["EVV"] += 97 * 4

        @block.sync
        def _(sy):
            sy.wait_ge(sSTS, ctr["STS"])
            sy.wait_ge(sLDS, ctr["LDS"])

        @block.gpsimd
        def _(g):
            g.wait_ge(sLDG, ctr["LDG"])

        @block.scalar
        def _(sc):
            sc.wait_ge(sLDK, ctr["LDK"])
            sc.wait_ge(sLDV, ctr["LDV"])
            sc.wait_ge(sLDF, ctr["LDF"])

    return nc


def host_inputs(inp, core):
    import ml_dtypes as mld
    x = np.ascontiguousarray(np.asarray(inp["L0_attn_x"])[core])
    m = {}
    m["x_in"] = x
    m["wq_t"] = np.ascontiguousarray(np.asarray(inp["L0_attn_Wq"]).T)
    m["wk_t"] = np.ascontiguousarray(np.asarray(inp["L0_attn_Wk"]).T)
    m["wv_t"] = np.ascontiguousarray(np.asarray(inp["L0_attn_Wv"]).T)
    m["wg_t"] = np.ascontiguousarray(np.asarray(inp["L0_attn_Wg"]).T)
    m["wp_t"] = np.ascontiguousarray(np.asarray(inp["L0_attn_Wp"]).T)
    m["w_gate"] = np.ascontiguousarray(np.asarray(inp["L0_ffn_w_gate"]))
    m["w_up"] = np.ascontiguousarray(np.asarray(inp["L0_ffn_w_up"]))
    m["w_down"] = np.ascontiguousarray(np.asarray(inp["L0_ffn_w_down"]))
    m["s_gate"] = np.ascontiguousarray(np.asarray(inp["L0_ffn_s_gate"]).reshape(64, 128).T)
    m["s_up"] = np.ascontiguousarray(np.asarray(inp["L0_ffn_s_up"]).reshape(64, 128).T)
    m["s_down"] = np.ascontiguousarray(np.asarray(inp["L0_ffn_s_down"]))
    m["ident_in"] = np.eye(128, dtype=mld.bfloat16)
    m["ones_in"] = np.ones((1, 128), dtype=np.float32)
    mk = np.zeros((128, 512), dtype=np.float32)
    ii = np.arange(256)
    pp = np.arange(128)[:, None]
    mk[:, 0:256] = (ii[None, :] >= pp).astype(np.float32)
    mk[:, 256:512] = (ii[None, :] >= pp + 128).astype(np.float32)
    m["masks_in"] = mk
    return m


from concourse.bass_utils import run_bass_kernel_spmd


def kernel(**inputs):
    nc = build(upto="E2", debug=False)
    in_maps = [host_inputs(inputs, core) for core in range(8)]
    res = run_bass_kernel_spmd(nc, in_maps, core_ids=list(range(8)))
    out = np.stack([np.asarray(res.results[i]["out"]) for i in range(8)], axis=0)
    return out


# version 08
# speedup vs baseline: 1.1640x; 1.1640x over previous
import contextlib
import numpy as np
import concourse.bass as bass
import concourse.mybir as mybir
from concourse.alu_op_type import AluOpType

F32 = mybir.dt.float32
BF16 = mybir.dt.bfloat16
I8 = mybir.dt.int8
AF = mybir.ActivationFunctionType
AX = mybir.AxisListType.X

D = 2048
NT = 4096
KD = 16
CH = 256
NCH = 16
INTER = 8192
LN_EPS = 1e-5
RMS_EPS = float(np.finfo(np.float32).eps)

PHASES = ["A", "B", "C", "D", "E1", "E2"]


def build(upto="E2", debug=False):
    nc = bass.Bass("TRN2", target_bir_lowering=False)
    upto_i = PHASES.index(upto)

    x_in = nc.dram_tensor("x_in", [NT, D], F32, kind="ExternalInput").ap()
    xt_in = nc.dram_tensor("xt_in", [D, NT], F32, kind="ExternalInput").ap()
    wq_t = nc.dram_tensor("wq_t", [D, D], F32, kind="ExternalInput").ap()
    wk_t = nc.dram_tensor("wk_t", [D, D], F32, kind="ExternalInput").ap()
    wv_t = nc.dram_tensor("wv_t", [D, D], F32, kind="ExternalInput").ap()
    wg_t = nc.dram_tensor("wg_t", [D, D], F32, kind="ExternalInput").ap()
    wp_t = nc.dram_tensor("wp_t", [D, D], F32, kind="ExternalInput").ap()
    w_gate = nc.dram_tensor("w_gate", [D, INTER], I8, kind="ExternalInput").ap()
    w_up = nc.dram_tensor("w_up", [D, INTER], I8, kind="ExternalInput").ap()
    w_down = nc.dram_tensor("w_down", [INTER, D], I8, kind="ExternalInput").ap()
    s_gate = nc.dram_tensor("s_gate", [128, INTER // 128], F32, kind="ExternalInput").ap()
    s_up = nc.dram_tensor("s_up", [128, INTER // 128], F32, kind="ExternalInput").ap()
    s_down = nc.dram_tensor("s_down", [D], F32, kind="ExternalInput").ap()
    ident_in = nc.dram_tensor("ident_in", [128, 128], BF16, kind="ExternalInput").ap()
    identf_in = nc.dram_tensor("identf_in", [128, 128], F32, kind="ExternalInput").ap()
    ones_in = nc.dram_tensor("ones_in", [1, 128], F32, kind="ExternalInput").ap()
    masks_in = nc.dram_tensor("masks_in", [128, 512], F32, kind="ExternalInput").ap()

    out_d = nc.dram_tensor("out", [NT, D], F32, kind="ExternalOutput").ap()

    def scratch(name, shape, dtype):
        return nc.dram_tensor(name, shape, dtype,
                              kind="ExternalOutput" if debug else "Internal").ap()

    stat_d = scratch("stat_d", [96, 128], F32)
    xnT_d = scratch("xnT_d", [D, NT], BF16)
    xn2T_d = scratch("xn2T_d", [D, NT], BF16)
    qT_d = scratch("qT_d", [D, NT], BF16)
    k_d = scratch("k_d", [NT, D], BF16)
    v_d = scratch("v_d", [NT, D], BF16)
    oT_d = scratch("oT_d", [D, NT], BF16)
    sout_d = scratch("sout_d", [NT, D], F32)
    fuse_d = scratch("fuse_d", [NT // 256, 128, INTER // 128, 256], BF16)

    xnT_v = xnT_d.rearrange("(k p) t -> p k t", p=128)
    xn2T_v = xn2T_d.rearrange("(k p) t -> p k t", p=128)
    qT_v = qT_d.rearrange("(k p) t -> p k t", p=128)
    oT_v = oT_d.rearrange("(k p) t -> p k t", p=128)
    k_v = k_d.rearrange("(j p) f -> p j f", p=128)
    v_v = v_d.rearrange("(j p) f -> p j f", p=128)
    x_v = x_in.rearrange("(j p) f -> p j f", p=128)
    xt_v = xt_in.rearrange("(k p) t -> p k t", p=128)
    out_v = out_d.rearrange("(j p) f -> p j f", p=128)
    sout_v = sout_d.rearrange("(j p) f -> p j f", p=128)
    wqt_v = wq_t.rearrange("(k p) f -> p k f", p=128)
    wkt_v = wk_t.rearrange("(k p) f -> p k f", p=128)
    wvt_v = wv_t.rearrange("(k p) f -> p k f", p=128)
    wgt_v = wg_t.rearrange("(k p) f -> p k f", p=128)
    wpt_v = wp_t.rearrange("(k p) f -> p k f", p=128)
    wgate_v = w_gate.rearrange("(k p) f -> p k f", p=128)
    wup_v = w_up.rearrange("(k p) f -> p k f", p=128)
    wdown_v = w_down.rearrange("(k p) f -> p k f", p=128)


    with contextlib.ExitStack() as _es:
        ident = _es.enter_context(nc.sbuf_tensor("ident", [128, 128], BF16))
        psum = _es.enter_context(nc.psum_tensor("psum", [128, 4096], F32))
        BAR = _es.enter_context(nc.semaphore("BAR"))
        sLDS = _es.enter_context(nc.semaphore("LDS"))
        sLDG = _es.enter_context(nc.semaphore("LDG"))
        sMM = _es.enter_context(nc.semaphore("MM"))
        sEVV = _es.enter_context(nc.semaphore("EVV"))
        sEVS = _es.enter_context(nc.semaphore("EVS"))
        sSTS = _es.enter_context(nc.semaphore("STS"))
        sLDK = _es.enter_context(nc.semaphore("LDK"))
        sLDV = _es.enter_context(nc.semaphore("LDV"))
        sLDF = _es.enter_context(nc.semaphore("LDF"))
        block = _es.enter_context(nc.Block())
        ctr = {"LDS": 0, "LDG": 0, "MM": 0, "EVV": 0, "EVS": 0, "STS": 0, "BAR": 0,
               "LDK": 0, "LDV": 0, "LDF": 0}

        def bank(b):
            return psum[:, b * 512:(b + 1) * 512]

        def region(b, off, w):
            return psum[:, b * 512 + off:b * 512 + off + w]

        def barrier():
            ctr["BAR"] += 5
            tgt = ctr["BAR"]
            snap = dict(ctr)

            def drain(eng, name):
                eng.wait_ge(sMM, snap["MM"])
                eng.wait_ge(sEVV, snap["EVV"])
                eng.wait_ge(sEVS, snap["EVS"])
                eng.wait_ge(sLDS, snap["LDS"])
                eng.wait_ge(sLDK, snap["LDK"])
                eng.wait_ge(sLDV, snap["LDV"])
                eng.wait_ge(sLDF, snap["LDF"])
                eng.wait_ge(sSTS, snap["STS"])
                eng.wait_ge(sLDG, snap["LDG"])

            for name in ["tensor", "vector", "scalar", "gpsimd", "sync"]:
                def mk(eng_name):
                    def body(eng):
                        drain(eng, eng_name)
                        eng.nop().then_inc(BAR, 1)
                        eng.wait_ge(BAR, tgt)
                    return body
                getattr(block, name)(mk(name))

        @block.sync
        def _(sy):
            sy.dma_start(out=ident[:], in_=ident_in[:]).then_inc(sLDS, 16)
        ctr["LDS"] += 16

        @block.tensor
        def _(t):
            t.wait_ge(sLDS, ctr["LDS"])

        _ab_es = contextlib.ExitStack()
        bw0 = _ab_es.enter_context(nc.sbuf_tensor("bw0", [128, KD, D], BF16))

        with (
            nc.sbuf_tensor("a_xb", [128, 2, 2, D], F32) as a_xb,
            nc.sbuf_tensor("a_sq", [128, D], F32) as a_sq,
            nc.sbuf_tensor("a_st", [128, 12, 32], F32) as a_st,
            nc.sbuf_tensor("a_pk", [128, 96], F32) as a_pk,
            nc.sbuf_tensor("a_stt", [96, 128], F32) as a_stt,
            nc.sbuf_tensor("a_srow", [1, 2, 512], F32) as a_srow,
            nc.sbuf_tensor("a_ones", [1, 128], F32) as a_ones,
            nc.sbuf_tensor("a_idf", [128, 128], F32) as a_idf,
            nc.sbuf_tensor("a_bst", [128, 3, NT], F32) as a_bst,
            nc.sbuf_tensor("a_xt", [128, 2, 2048], F32) as a_xt,
            nc.sbuf_tensor("a_tmp", [128, 2, 2048], F32) as a_tmp,
            nc.sbuf_tensor("a_xnb", [128, 2, 2048], BF16) as a_xnb,
            nc.sbuf_tensor("a_x2b", [128, 2, 2048], BF16) as a_x2b,
        ):
            NTILE = 32
            lds0, ldg0, mm0, evv0, evs0, sts0, ldv0 = (ctr["LDS"], ctr["LDG"], ctr["MM"],
                                                       ctr["EVV"], ctr["EVS"], ctr["STS"],
                                                       ctr["LDV"])
            lds_p1 = lambda li: lds0 + 16 * (2 + li + 1)
            lds_sr = lambda j: lds0 + 16 * (18 + j + 1)
            lds_p2 = lambda i: lds0 + 16 * (42 + i + 1)
            evv_red = lambda ti: evv0 + ti + 1
            evv_ch = lambda o: evv0 + 32 + o
            evv_sub = lambda i: evv0 + 47 + 2 * i + 1
            evv_mul = lambda i: evv0 + 47 + 2 * i + 2
            evs_sq = lambda ti: evs0 + ti + 1
            evs_bc = lambda j: evs0 + 34 + j + 1
            mm_bc = lambda j: mm0 + j + 2
            sts_xn = lambda i: sts0 + 16 * (2 * i + 2)
            sts_x2 = lambda i: sts0 + 16 * (2 * i + 3)
            ldv_tt = lambda i: ldv0 + i + 1

            @block.sync
            def _(sy):
                sy.wait_ge(sLDS, lds0)
                sy.dma_start(out=a_idf[:], in_=identf_in[:]).then_inc(sLDS, 16)
                sy.wait_ge(sLDS, lds0 + 16)
                sy.dma_start(out=a_ones[:], in_=ones_in[:]).then_inc(sLDS, 16)
                for li in range(16):
                    if li >= 2:
                        sy.wait_ge(sEVV, evv_red(2 * (li - 2) + 1))
                        sy.wait_ge(sEVS, evs_sq(2 * (li - 2) + 1))
                    sy.wait_ge(sLDS, lds_p1(li) - 16)
                    sy.dma_start(out=a_xb[:, li % 2],
                                 in_=x_v[:, 2 * li:2 * li + 2, :]).then_inc(sLDS, 16)
                flat = stat_d.rearrange("a p -> (a p)")
                for j in range(24):
                    s3, ts = divmod(j, 8)
                    if j < 2:
                        sy.wait_ge(sSTS, sts0 + 16)
                    else:
                        sy.wait_ge(sMM, mm_bc(j - 2))
                    sy.wait_ge(sLDS, lds_sr(j) - 16)
                    sy.dma_start(out=a_srow[:, j % 2, :],
                                 in_=flat[None, s3 * 4096 + ts * 512:
                                          s3 * 4096 + ts * 512 + 512]).then_inc(sLDS, 16)
                for i in range(NTILE):
                    ts2, k = divmod(i, 16)
                    if i == 0:
                        sy.wait_ge(sEVS, evs_bc(23))
                    if i >= 2:
                        sy.wait_ge(sEVV, evv_sub(i - 2))
                    sy.wait_ge(sLDS, lds_p2(i) - 16)
                    sy.dma_start(out=a_xt[:, i % 2],
                                 in_=xt_v[:, k, ts2 * 2048:(ts2 + 1) * 2048]
                                 ).then_inc(sLDS, 16)
            ctr["LDS"] += 16 * 74

            @block.gpsimd
            def _(g):
                g.wait_ge(sLDG, ldg0)
                g.dma_start(out=bw0[:], in_=wqt_v[:]).then_inc(sLDG, 16)
                for i in range(NTILE):
                    g.wait_ge(sEVV, evv_sub(i))
                    if i >= 2:
                        g.wait_ge(sSTS, sts_x2(i - 2))
                    ts2, k = divmod(i, 16)
                    g.tensor_tensor(a_x2b[:, i % 2], a_tmp[:, i % 2],
                                    a_bst[:, 2, ts2 * 2048:(ts2 + 1) * 2048],
                                    AluOpType.mult).then_inc(sLDV, 1)
            ctr["LDG"] += 16
            ctr["LDV"] += NTILE

            @block.scalar
            def _(sc):
                for ti in range(NTILE):
                    sc.wait_ge(sLDS, lds_p1(ti // 2))
                    sc.activation(a_sq[:], a_xb[:, (ti // 2) % 2, ti % 2, :], AF.Square,
                                  accum_out=a_st[:, 1, ti:ti + 1]).then_inc(sEVS, 1)
                sc.wait_ge(sEVV, evv_ch(5))
                sc.activation(a_st[:, 6], a_st[:, 5], AF.Sqrt).then_inc(sEVS, 1)
                sc.wait_ge(sEVV, evv_ch(9))
                sc.activation(a_st[:, 9], a_st[:, 8], AF.Sqrt).then_inc(sEVS, 1)
                sc.wait_ge(sEVV, evv_ch(15))
                sc.wait_ge(sSTS, sts0)
                sc.dma_start(out=stat_d[:], in_=a_stt[:]).then_inc(sSTS, 16)
                for j in range(24):
                    s3, ts = divmod(j, 8)
                    sc.wait_ge(sMM, mm_bc(j))
                    sc.activation(a_bst[:, s3, ts * 512:(ts + 1) * 512],
                                  bank(j % 8), AF.Copy).then_inc(sEVS, 1)
                for i in range(NTILE):
                    ts2, k = divmod(i, 16)
                    sc.wait_ge(sEVV, evv_mul(i))
                    sc.wait_ge(sSTS, sts_xn(i) - 16)
                    sc.dma_start(out=xnT_v[:, k, ts2 * 2048:(ts2 + 1) * 2048],
                                 in_=a_xnb[:, i % 2]).then_inc(sSTS, 16)
                    sc.wait_ge(sLDV, ldv_tt(i))
                    sc.wait_ge(sSTS, sts_x2(i) - 16)
                    sc.dma_start(out=xn2T_v[:, k, ts2 * 2048:(ts2 + 1) * 2048],
                                 in_=a_x2b[:, i % 2]).then_inc(sSTS, 16)
            ctr["EVS"] += 58
            ctr["STS"] += 16 * 65

            @block.vector
            def _(v):
                st = a_st
                for li in range(16):
                    v.wait_ge(sLDS, lds_p1(li))
                    for jj in range(2):
                        v.reduce_sum(st[:, 0, 2 * li + jj:2 * li + jj + 1],
                                     a_xb[:, li % 2, jj, :], axis=AX).then_inc(sEVV, 1)
                v.wait_ge(sEVV, evv0 + NTILE)
                o = 0
                def step(ins):
                    nonlocal o
                    o += 1
                    ins.then_inc(sEVV, 1)
                    v.wait_ge(sEVV, evv_ch(o))
                step(v.tensor_scalar(st[:, 2], st[:, 0], 1.0 / D, None, AluOpType.mult))
                v.wait_ge(sEVS, evs0 + NTILE)
                step(v.tensor_scalar(st[:, 3], st[:, 1], 1.0 / D, None, AluOpType.mult))
                step(v.tensor_tensor(st[:, 4], st[:, 2], st[:, 2], AluOpType.mult))
                step(v.tensor_tensor(st[:, 4], st[:, 3], st[:, 4], AluOpType.subtract))
                step(v.tensor_scalar(st[:, 5], st[:, 4], LN_EPS, None, AluOpType.add))
                v.wait_ge(sEVS, evs0 + NTILE + 1)
                step(v.reciprocal(st[:, 7], st[:, 6]))
                step(v.reciprocal(st[:, 8], st[:, 5]))
                step(v.tensor_tensor(st[:, 8], st[:, 4], st[:, 8], AluOpType.mult))
                step(v.tensor_scalar(st[:, 8], st[:, 8], RMS_EPS, None, AluOpType.add))
                v.wait_ge(sEVS, evs0 + NTILE + 2)
                step(v.reciprocal(st[:, 10], st[:, 9]))
                step(v.tensor_tensor(st[:, 11], st[:, 7], st[:, 10], AluOpType.mult))
                assert o == 11
                step(v.tensor_copy(a_pk[:, 0:32], st[:, 2]))
                step(v.tensor_copy(a_pk[:, 32:64], st[:, 7]))
                step(v.tensor_copy(a_pk[:, 64:96], st[:, 11]))
                v.wait_ge(sMM, mm0 + 1)
                v.tensor_copy(a_stt[:], psum[0:96, 0:128]).then_inc(sEVV, 1)
                v.wait_ge(sEVV, evv_ch(15))
                for i in range(NTILE):
                    ts2, k = divmod(i, 16)
                    v.wait_ge(sLDS, lds_p2(i))
                    if i >= 2:
                        v.wait_ge(sLDV, ldv_tt(i - 2))
                        v.wait_ge(sSTS, sts_xn(i - 2))
                    v.tensor_tensor(a_tmp[:, i % 2], a_xt[:, i % 2],
                                    a_bst[:, 0, ts2 * 2048:(ts2 + 1) * 2048],
                                    AluOpType.subtract).then_inc(sEVV, 1)
                    v.wait_ge(sEVV, evv_sub(i))
                    v.tensor_tensor(a_xnb[:, i % 2], a_tmp[:, i % 2],
                                    a_bst[:, 1, ts2 * 2048:(ts2 + 1) * 2048],
                                    AluOpType.mult).then_inc(sEVV, 1)
            ctr["EVV"] += 47 + 2 * NTILE

            @block.tensor
            def _(t):
                t.wait_ge(sEVV, evv_ch(14))
                t.wait_ge(sLDS, lds0 + 16)
                t.matmul(psum[0:96, 0:128], a_pk[:], a_idf[:],
                         start=True, stop=True).then_inc(sMM, 1)
                for j in range(24):
                    t.wait_ge(sLDS, lds_sr(j))
                    if j == 0:
                        t.wait_ge(sEVV, evv_ch(15))
                        t.wait_ge(sLDS, lds0 + 32)
                    if j >= 8:
                        t.wait_ge(sEVS, evs_bc(j - 8))
                    t.matmul(bank(j % 8), a_ones[:], a_srow[:, j % 2, :],
                             start=True, stop=True).then_inc(sMM, 1)
            ctr["MM"] += 25

        barrier()

        if upto_i >= 1:
            with (
                nc.sbuf_tensor("b_w1", [128, KD, D], BF16) as b_w1,
                nc.sbuf_tensor("b_x", [128, 2, KD, 512], BF16) as b_x,
                nc.sbuf_tensor("b_qstg", [128, 2, 512], BF16) as b_qstg,
                nc.sbuf_tensor("b_kstg", [128, 2, D], BF16) as b_kstg,
            ):
                b_bufs = [bw0, b_w1, bw0]
                lds0, ldg0, mm0, evs0, sts0 = (ctr["LDS"], ctr["LDG"], ctr["MM"],
                                               ctr["EVS"], ctr["STS"])
                mm_g = lambda p, ts, g: mm0 + 128 * p + 16 * ts + g + 1
                evs_g = lambda p, ts, g: evs0 + 128 * p + 16 * ts + g + 1

                sts_sched = []
                stv = sts0
                store_of_stream = {}
                for p in range(3):
                    for ts in range(8):
                        if p == 0:
                            for f in range(16):
                                stv += 16
                                store_of_stream[("q", ts, f)] = stv
                        else:
                            for ib in range(4):
                                stv += 16
                                store_of_stream[("kv", p, ts, ib)] = stv

                @block.gpsimd
                def _(g):
                    g.wait_ge(sLDG, ldg0)
                    g.dma_start(out=b_w1[:], in_=wkt_v[:]).then_inc(sLDG, 16)
                    g.wait_ge(sMM, mm_g(0, 7, 15))
                    g.wait_ge(sLDG, ldg0 + 16)
                    g.dma_start(out=bw0[:], in_=wvt_v[:]).then_inc(sLDG, 16)
                ctr["LDG"] += 32

                @block.sync
                def _(sy):
                    for p in range(3):
                        for ts in range(8):
                            idx = 8 * p + ts
                            if idx >= 2:
                                sy.wait_ge(sMM, mm0 + 16 * (idx - 1))
                            sy.wait_ge(sLDS, lds0 + 16 * idx)
                            sy.dma_start(out=b_x[:, idx % 2],
                                         in_=xn2T_v[:, :, ts * 512:(ts + 1) * 512]
                                         ).then_inc(sLDS, 16)
                ctr["LDS"] += 16 * 24

                @block.tensor
                def _(t):
                    for p in range(3):
                        wbuf = b_bufs[p][:]
                        if p > 0:
                            t.wait_ge(sLDG, ldg0 + 16 * p)
                        for ts in range(8):
                            idx = 8 * p + ts
                            t.wait_ge(sLDS, lds0 + 16 * (idx + 1))
                            xbuf = b_x[:, idx % 2]
                            for g in range(16):
                                gi = 128 * p + 16 * ts + g
                                if gi >= 4:
                                    pi, r = divmod(gi - 4, 128)
                                    ti, g2 = divmod(r, 16)
                                    t.wait_ge(sEVS, evs_g(pi, ti, g2))
                                ins = None
                                if p == 0:
                                    for k in range(KD):
                                        ins = t.matmul(bank(g % 4),
                                                       wbuf[:, k, g * 128:(g + 1) * 128],
                                                       xbuf[:, k, :],
                                                       start=(k == 0), stop=(k == KD - 1))
                                else:
                                    ib, ot = divmod(g, 4)
                                    for k in range(KD):
                                        ins = t.matmul(bank(g % 4),
                                                       xbuf[:, k, ib * 128:(ib + 1) * 128],
                                                       wbuf[:, k, ot * 512:(ot + 1) * 512],
                                                       start=(k == 0), stop=(k == KD - 1))
                                ins.then_inc(sMM, 1)
                ctr["MM"] += 3 * 128

                @block.scalar
                def _(sc):
                    for p in range(3):
                        for ts in range(8):
                            for g in range(16):
                                sc.wait_ge(sMM, mm_g(p, ts, g))
                                if p == 0:
                                    sf = 16 * ts + g
                                    if sf >= 2:
                                        sc.wait_ge(sSTS, store_of_stream[("q", (sf - 2) // 16,
                                                                          (sf - 2) % 16)])
                                    sc.activation(b_qstg[:, sf % 2], bank(g % 4),
                                                  AF.Silu).then_inc(sEVS, 1)
                                    sc.wait_ge(sEVS, evs_g(p, ts, g))
                                    sc.wait_ge(sSTS, store_of_stream[("q", ts, g)] - 16)
                                    sc.dma_start(out=qT_v[:, g, ts * 512:(ts + 1) * 512],
                                                 in_=b_qstg[:, sf % 2]).then_inc(sSTS, 16)
                                else:
                                    ib, ot = divmod(g, 4)
                                    ks = (p - 1) * 32 + ts * 4 + ib
                                    if ot == 0 and ks >= 2:
                                        k2 = ks - 2
                                        p2 = k2 // 32 + 1
                                        t2, i2 = divmod(k2 % 32, 4)
                                        sc.wait_ge(sSTS, store_of_stream[("kv", p2, t2, i2)])
                                    sc.activation(b_kstg[:, ks % 2, ot * 512:(ot + 1) * 512],
                                                  bank(g % 4), AF.Silu).then_inc(sEVS, 1)
                                    if ot == 3:
                                        sc.wait_ge(sEVS, evs_g(p, ts, g))
                                        sc.wait_ge(sSTS, store_of_stream[("kv", p, ts, ib)] - 16)
                                        dst = k_v if p == 1 else v_v
                                        sc.dma_start(out=dst[:, ts * 4 + ib, :],
                                                     in_=b_kstg[:, ks % 2]).then_inc(sSTS, 16)
                ctr["EVS"] += 3 * 128
                ctr["STS"] = stv

            barrier()
        _ab_es.close()

        if upto_i >= 2:
            with (
                nc.sbuf_tensor("c_kv", [128, KD, D], F32) as c_kv,
                nc.sbuf_tensor("c_q", [128, 2, KD, CH], BF16) as c_q,
                nc.sbuf_tensor("c_k", [128, 2, 2, D], BF16) as c_k,
                nc.sbuf_tensor("c_v", [128, 2, D], BF16) as c_v,
                nc.sbuf_tensor("c_kT", [128, KD, CH], BF16) as c_kT,
                nc.sbuf_tensor("c_sT", [128, 2, CH], BF16) as c_sT,
                nc.sbuf_tensor("c_oT", [128, 2, KD, CH], BF16) as c_oT,
                nc.sbuf_tensor("c_kvbf", [128, 4, 1024], BF16) as c_kvbf,
                nc.sbuf_tensor("masks", [128, 512], F32) as masks,
            ):
                lds0, mm0, evv0, evs0, sts0 = (ctr["LDS"], ctr["MM"], ctr["EVV"],
                                               ctr["EVS"], ctr["STS"])
                def mm_base(t):
                    return mm0 + (0 if t == 0 else 42 + 74 * (t - 1))

                def mm_end(t):
                    return mm0 + 42 + 74 * t if t >= 0 else mm0

                def mm_a(t, g):
                    return mm_base(t) + g + 1

                def mm_b(t, j):
                    return mm_base(t) + 8 + j + 1

                def mm_ik(t, h, k):
                    return mm_base(t) + 10 + 24 * h + k + 1

                def mm_ie(t, h, e):
                    if t == 0:
                        return mm_base(t) + 10 + 8 * h + e + 1
                    return mm_base(t) + 10 + 24 * h + 16 + e + 1

                def mm_d(t, k):
                    if t == 0:
                        return mm_base(t) + 26 + k + 1
                    return mm_base(t) + 58 + k + 1

                evv_c = lambda t, o: evv0 + 42 * t + o
                evs_stage = lambda t, h, k: evs0 + 32 * (t - 1) + 16 * h + k + 1
                ldk0, ldv0 = ctr["LDK"], ctr["LDV"]
                sts_o = lambda t: sts0 + 16 * (t + 1)

                @block.sync
                def _(sy):
                    for t in range(NCH):
                        if t >= 2:
                            sy.wait_ge(sMM, mm_end(t - 2))
                        sy.wait_ge(sLDS, lds0 + 16 * t)
                        sy.dma_start(out=c_q[:, t % 2],
                                     in_=qT_v[:, :, t * CH:(t + 1) * CH]).then_inc(sLDS, 16)
                        sy.wait_ge(sLDK, ldk0 + 16 * t)
                        sy.dma_start(out=c_k[:, t % 2],
                                     in_=k_v[:, 2 * t:2 * t + 2, :]).then_inc(sLDK, 16)
                        if t >= 1:
                            sy.wait_ge(sMM, mm_end(t - 1))
                        sy.wait_ge(sLDV, ldv0 + 16 * t)
                        sy.dma_start(out=c_v[:],
                                     in_=v_v[:, 2 * t:2 * t + 2, :]).then_inc(sLDV, 16)
                        if t >= 1:
                            sy.wait_ge(sEVV, evv_c(t - 1, 26))
                            sy.wait_ge(sSTS, sts_o(t - 1) - 16)
                            sy.dma_start(out=oT_v[:, :, (t - 1) * CH:t * CH],
                                         in_=c_oT[:, (t - 1) % 2]).then_inc(sSTS, 16)
                    sy.wait_ge(sEVV, evv_c(NCH - 1, 26))
                    sy.wait_ge(sSTS, sts_o(NCH - 1) - 16)
                    sy.dma_start(out=oT_v[:, :, (NCH - 1) * CH:NCH * CH],
                                 in_=c_oT[:, (NCH - 1) % 2]).then_inc(sSTS, 16)
                ctr["LDS"] += 16 * NCH
                ctr["LDK"] += 16 * NCH
                ctr["LDV"] += 16 * NCH
                ctr["STS"] += 16 * NCH

                ldgC0 = ctr["LDG"]

                @block.gpsimd
                def _(g):
                    g.wait_ge(sLDG, ldgC0)
                    g.dma_start(out=masks[:], in_=masks_in[:]).then_inc(sLDG, 16)
                ctr["LDG"] += 16

                @block.scalar
                def _(sc):
                    for t in range(1, NCH):
                        for h in range(2):
                            for k in range(KD):
                                idx = 16 * h + k
                                sc.wait_ge(sEVV, evv_c(t - 1, 26 + k + 1))
                                if idx >= 4:
                                    h2, k2 = divmod(idx - 4, 16)
                                    sc.wait_ge(sMM, mm_ik(t, h2, k2))
                                elif t >= 2:
                                    h2, k2 = divmod(idx - 4 + 32, 16)
                                    sc.wait_ge(sMM, mm_ik(t - 1, h2, k2))
                                sc.activation(c_kvbf[:, idx % 4],
                                              c_kv[:, k, h * 1024:(h + 1) * 1024],
                                              AF.Copy).then_inc(sEVS, 1)
                ctr["EVS"] += 32 * (NCH - 1)

                @block.tensor
                def _(t_):
                    for t in range(NCH):
                        t_.wait_ge(sLDS, lds0 + 16 * (t + 1))
                        t_.wait_ge(sLDK, ldk0 + 16 * (t + 1))
                        for g in range(8):
                            if t >= 1:
                                t_.wait_ge(sEVV, evv_c(t - 1, 41 if g < 4 else 42))
                            j, ks = divmod(4 * g, 16)
                            ins = None
                            for kk in range(4):
                                k = ks + kk
                                ins = t_.matmul(region(g, kk * 128, 128),
                                                c_k[:, t % 2, j, k * 128:(k + 1) * 128],
                                                ident[:], start=True, stop=True)
                            ins.then_inc(sMM, 1)
                        for j in range(2):
                            t_.wait_ge(sEVV, evv_c(t, 4 * (j + 1)))
                            ins = None
                            for k in range(KD):
                                ins = t_.matmul(region(j, 0, CH),
                                                c_kT[:, k, j * 128:(j + 1) * 128],
                                                c_q[:, t % 2, k, :],
                                                start=(k == 0), stop=(k == KD - 1))
                            ins.then_inc(sMM, 1)
                        t_.wait_ge(sEVV, evv_c(t, 10))
                        t_.wait_ge(sLDV, ldv0 + 16 * (t + 1))
                        for h in range(2):
                            if h == 1:
                                t_.wait_ge(sEVV, evv_c(t, 18))
                            if t >= 1:
                                for k in range(KD):
                                    idx = 16 * h + k
                                    t_.wait_ge(sEVS, evs_stage(t, h, k))
                                    for b in range(8):
                                        ins = t_.matmul(
                                            region(b, 0, CH),
                                            c_kvbf[:, idx % 4, b * 128:(b + 1) * 128],
                                            c_q[:, t % 2, k, :],
                                            start=(k == 0), stop=False,
                                            skip_group_check=True)
                                    ins.then_inc(sMM, 1)
                            for b in range(8):
                                e = 8 * h + b
                                ins = None
                                for j in range(2):
                                    ins = t_.matmul(region(b, 0, CH),
                                                    c_v[:, j, e * 128:(e + 1) * 128],
                                                    c_sT[:, j, :],
                                                    start=(t == 0 and j == 0),
                                                    stop=(j == 1),
                                                    skip_group_check=True)
                                ins.then_inc(sMM, 1)
                        t_.wait_ge(sEVV, evv_c(t, 26))
                        for k in range(KD):
                            gk = k % 2
                            if k >= 2:
                                t_.wait_ge(sEVV, evv_c(t, 26 + k - 1))
                            ins = None
                            for et in range(4):
                                for j in range(2):
                                    ins = t_.matmul(bank(gk * 4 + et),
                                                    c_k[:, t % 2, j, k * 128:(k + 1) * 128],
                                                    c_v[:, j, et * 512:(et + 1) * 512],
                                                    start=(j == 0), stop=(j == 1),
                                                    skip_group_check=True)
                            ins.then_inc(sMM, 1)
                ctr["MM"] += 42 + 74 * (NCH - 1)

                @block.vector
                def _(v):
                    v.wait_ge(sLDG, ldgC0 + 16)
                    for t in range(NCH):
                        for g in range(8):
                            v.wait_ge(sMM, mm_a(t, g))
                            j, ks = divmod(4 * g, 16)
                            v.tensor_copy(c_kT[:, ks:ks + 4, j * 128:(j + 1) * 128],
                                          bank(g).rearrange("p (a c) -> p a c", a=4)
                                          ).then_inc(sEVV, 1)
                        for j in range(2):
                            v.wait_ge(sMM, mm_b(t, j))
                            v.tensor_tensor(c_sT[:, j, :], region(j, 0, CH),
                                            masks[:, j * 256:(j + 1) * 256],
                                            AluOpType.mult).then_inc(sEVV, 1)
                        if t >= 2:
                            v.wait_ge(sSTS, sts_o(t - 2))
                        for h in range(2):
                            for b in range(8):
                                e = 8 * h + b
                                v.wait_ge(sMM, mm_ie(t, h, b))
                                v.tensor_copy(c_oT[:, t % 2, e, :],
                                              region(b, 0, CH)).then_inc(sEVV, 1)
                        for k in range(KD):
                            gk = k % 2
                            v.wait_ge(sMM, mm_d(t, k))
                            if t >= 1:
                                v.wait_ge(sEVS, evs_stage(t, 1, k))
                            span = psum[:, gk * 2048:(gk + 1) * 2048]
                            dst = c_kv[:, k, :]
                            if t == 0:
                                ins = v.tensor_copy(dst, span)
                            else:
                                ins = v.tensor_tensor(dst, span, dst, AluOpType.add)
                            ins.then_inc(sEVV, 1)
                ctr["EVV"] += 42 * NCH

            barrier()

        if upto_i >= 3:
            with (
                nc.sbuf_tensor("d_wg", [128, KD, D], BF16) as d_wg,
                nc.sbuf_tensor("d_wp", [128, KD, D], BF16) as d_wp,
                nc.sbuf_tensor("d_oT", [128, 2, KD, 256], BF16) as d_oT,
                nc.sbuf_tensor("d_x2", [128, 2, KD, 256], BF16) as d_x2,
                nc.sbuf_tensor("d_og", [128, 2, KD, 256], BF16) as d_og,
                nc.sbuf_tensor("d_sig", [128, 2, 256], F32) as d_sig,
                nc.sbuf_tensor("d_sstg", [128, 2, D], F32) as d_sstg,
            ):
                lds0, ldg0, mm0, evv0, evs0, sts0 = (ctr["LDS"], ctr["LDG"], ctr["MM"],
                                                     ctr["EVV"], ctr["EVS"], ctr["STS"])
                NSL = 16
                mm_pg = lambda t, g: mm0 + 24 * t + g + 1
                mm_ps = lambda t, s: mm0 + 24 * t + 16 + s + 1
                evs_sg = lambda t, g: evs0 + 16 * t + g + 1
                evv_og = lambda t, g: evv0 + 24 * t + g + 1
                evv_ev = lambda t, s: evv0 + 24 * t + 16 + s + 1
                sts_d = lambda t, i: sts0 + 32 * t + 16 * (i + 1)

                @block.gpsimd
                def _(g):
                    g.wait_ge(sLDG, ldg0)
                    g.dma_start(out=d_wg[:, :, 0:1024], in_=wgt_v[:, :, 0:1024]).then_inc(sLDG, 16)
                    g.wait_ge(sLDG, ldg0 + 16)
                    g.dma_start(out=d_wg[:, :, 1024:2048], in_=wgt_v[:, :, 1024:2048]).then_inc(sLDG, 16)
                    g.wait_ge(sLDG, ldg0 + 32)
                    g.dma_start(out=d_wp[:], in_=wpt_v[:]).then_inc(sLDG, 16)
                ctr["LDG"] += 48

                @block.sync
                def _(sy):
                    for t in range(NSL):
                        if t >= 2:
                            sy.wait_ge(sEVV, evv_og(t - 2, 15))
                            sy.wait_ge(sMM, mm_pg(t - 2, 15))
                        sy.wait_ge(sLDS, lds0 + 32 * t)
                        sy.dma_start(out=d_oT[:, t % 2],
                                     in_=oT_v[:, :, t * 256:(t + 1) * 256]).then_inc(sLDS, 16)
                        sy.wait_ge(sLDS, lds0 + 32 * t + 16)
                        sy.dma_start(out=d_x2[:, t % 2],
                                     in_=xn2T_v[:, :, t * 256:(t + 1) * 256]).then_inc(sLDS, 16)
                        if t >= 1:
                            for i in range(2):
                                sy.wait_ge(sEVV, evv_ev(t - 1, 4 * i + 3))
                                sy.wait_ge(sSTS, sts_d(t - 1, i) - 16)
                                sy.dma_start(out=sout_v[:, (t - 1) * 2 + i, :],
                                             in_=d_sstg[:, i]).then_inc(sSTS, 16)
                    for i in range(2):
                        sy.wait_ge(sEVV, evv_ev(NSL - 1, 4 * i + 3))
                        sy.wait_ge(sSTS, sts_d(NSL - 1, i) - 16)
                        sy.dma_start(out=sout_v[:, (NSL - 1) * 2 + i, :],
                                     in_=d_sstg[:, i]).then_inc(sSTS, 16)
                ctr["LDS"] += 32 * NSL
                ctr["STS"] += 32 * NSL

                @block.tensor
                def _(t_):
                    t_.wait_ge(sLDG, ldg0 + 16)
                    for t in range(NSL):
                        t_.wait_ge(sLDS, lds0 + 32 * t + 32)
                        if t == 0:
                            pass
                        for g in range(16):
                            if t == 0 and g == 8:
                                t_.wait_ge(sLDG, ldg0 + 32)
                            b = g % 8
                            if g < 8:
                                if t >= 1:
                                    t_.wait_ge(sEVV, evv_ev(t - 1, b + 1))
                            else:
                                t_.wait_ge(sEVS, evs_sg(t, g - 8))
                            ins = None
                            for k in range(KD):
                                ins = t_.matmul(region(b, 0, 256),
                                                d_wg[:, k, g * 128:(g + 1) * 128],
                                                d_x2[:, t % 2, k, :],
                                                start=(k == 0), stop=(k == KD - 1))
                            ins.then_inc(sMM, 1)
                        if t == 0:
                            t_.wait_ge(sLDG, ldg0 + 48)
                        t_.wait_ge(sEVV, evv_og(t, 15))
                        for s in range(8):
                            i, ot = divmod(s, 4)
                            b = s % 8
                            t_.wait_ge(sEVS, evs_sg(t, 8 + b))
                            ins = None
                            for e in range(KD):
                                ins = t_.matmul(bank(b),
                                                d_og[:, t % 2, e, i * 128:(i + 1) * 128],
                                                d_wp[:, e, ot * 512:(ot + 1) * 512],
                                                start=(e == 0), stop=(e == KD - 1))
                            ins.then_inc(sMM, 1)
                ctr["MM"] += 24 * NSL

                @block.scalar
                def _(sc):
                    for t in range(NSL):
                        for g in range(16):
                            sc.wait_ge(sMM, mm_pg(t, g))
                            if g >= 2 or t >= 1:
                                g2 = g - 2 if g >= 2 else g + 14
                                t2 = t if g >= 2 else t - 1
                                sc.wait_ge(sEVV, evv_og(t2, g2))
                            sc.activation(d_sig[:, g % 2], region(g % 8, 0, 256),
                                          AF.Sigmoid).then_inc(sEVS, 1)
                ctr["EVS"] += 16 * NSL

                @block.vector
                def _(v):
                    for t in range(NSL):
                        if t >= 2:
                            v.wait_ge(sMM, mm_ps(t - 2, 8))
                        for g in range(16):
                            v.wait_ge(sEVS, evs_sg(t, g))
                            v.tensor_tensor(d_og[:, t % 2, g, :], d_oT[:, t % 2, g, :],
                                            d_sig[:, g % 2], AluOpType.mult).then_inc(sEVV, 1)
                        for s in range(8):
                            i, ot = divmod(s, 4)
                            v.wait_ge(sMM, mm_ps(t, s))
                            if t >= 1 and s % 4 == 0:
                                v.wait_ge(sSTS, sts_d(t - 1, i))
                            v.tensor_copy(d_sstg[:, i, ot * 512:(ot + 1) * 512],
                                          bank(s % 8)).then_inc(sEVV, 1)
                ctr["EVV"] += 24 * NSL

            barrier()

        if upto_i >= 4:
            with (
                nc.sbuf_tensor("e_wg", [128, 2, KD, 1024], BF16) as e_wg,
                nc.sbuf_tensor("e_wu", [128, 2, KD, 1024], BF16) as e_wu,
                nc.sbuf_tensor("e_x", [128, 2, KD, 512], BF16) as e_x,
                nc.sbuf_tensor("e_tg", [128, 2, 512], F32) as e_tg,
                nc.sbuf_tensor("e_tg2", [128, 2, 512], F32) as e_tg2,
                nc.sbuf_tensor("e_tu", [128, 2, 512], F32) as e_tu,
                nc.sbuf_tensor("e_fstg", [128, 4, 512], BF16) as e_fstg,
                nc.sbuf_tensor("sg_sb", [128, 64], F32) as sg_sb,
                nc.sbuf_tensor("su_sb", [128, 64], F32) as su_sb,
            ):
                lds0, ldg0, mm0, evv0, evs0, sts0 = (ctr["LDS"], ctr["LDG"], ctr["MM"],
                                                     ctr["EVV"], ctr["EVS"], ctr["STS"])
                NP = 8
                mm_e = lambda idx, g: mm0 + 16 * idx + g + 1
                evv_e = lambda idx, f, j: evv0 + 24 * idx + 3 * f + j + 1
                evs_e = lambda idx, f: evs0 + 8 * idx + f + 1
                sts_e = lambda sf: sts0 + 32 * (sf + 1)
                ldg_e = lambda pc: ldg0 + 32 + 32 * (pc + 1)

                @block.gpsimd
                def _(g):
                    g.wait_ge(sLDG, ldg0)
                    g.dma_start(out=sg_sb[:], in_=s_gate[:]).then_inc(sLDG, 16)
                    g.wait_ge(sLDG, ldg0 + 16)
                    g.dma_start(out=su_sb[:], in_=s_up[:]).then_inc(sLDG, 16)
                    for pc in range(NP):
                        if pc >= 2:
                            g.wait_ge(sMM, mm0 + 16 * 8 * (pc - 1))
                        g.wait_ge(sLDG, ldg0 + 32 + 32 * pc)
                        g.dma_start(out=e_wg[:, pc % 2],
                                    in_=wgate_v[:, :, pc * 1024:(pc + 1) * 1024]
                                    ).then_inc(sLDG, 16)
                        g.wait_ge(sLDG, ldg0 + 32 + 32 * pc + 16)
                        g.dma_start(out=e_wu[:, pc % 2],
                                    in_=wup_v[:, :, pc * 1024:(pc + 1) * 1024]
                                    ).then_inc(sLDG, 16)
                ctr["LDG"] += 32 + 32 * NP

                @block.sync
                def _(sy):
                    for pc in range(NP):
                        for ts in range(8):
                            idx = 8 * pc + ts
                            if idx >= 2:
                                sy.wait_ge(sMM, mm0 + 16 * (idx - 1))
                            sy.wait_ge(sLDS, lds0 + 16 * idx)
                            sy.dma_start(out=e_x[:, idx % 2],
                                         in_=xnT_v[:, :, ts * 512:(ts + 1) * 512]
                                         ).then_inc(sLDS, 16)
                ctr["LDS"] += 16 * 64
                ctr["STS"] += 32 * 512

                @block.tensor
                def _(t_):
                    for pc in range(NP):
                        t_.wait_ge(sLDG, ldg_e(pc))
                        for ts in range(8):
                            idx = 8 * pc + ts
                            t_.wait_ge(sLDS, lds0 + 16 * (idx + 1))
                            for g in range(16):
                                f = g // 2
                                gi = 16 * idx + g
                                if gi >= 4:
                                    i2, g2 = divmod(gi - 4, 16)
                                    f2, is_up = divmod(g2, 2)
                                    t_.wait_ge(sEVV, evv_e(i2, f2, 1 + is_up))
                                wsrc = e_wg if g % 2 == 0 else e_wu
                                ins = None
                                for k in range(KD):
                                    ins = t_.matmul(bank(g % 4),
                                                    wsrc[:, pc % 2, k, f * 128:(f + 1) * 128],
                                                    e_x[:, idx % 2, k, :],
                                                    start=(k == 0), stop=(k == KD - 1))
                                ins.then_inc(sMM, 1)
                ctr["MM"] += 16 * 64

                @block.scalar
                def _(sc):
                    def fstore(sf):
                        i2, f2 = divmod(sf, 8)
                        pc2, ts2 = divmod(i2, 8)
                        kf = pc2 * 8 + f2
                        sc.wait_ge(sEVV, evv_e(i2, f2, 2))
                        sc.wait_ge(sSTS, sts_e(sf) - 32)
                        sc.dma_start(out=fuse_d[2 * ts2, :, kf, :],
                                     in_=e_fstg[:, sf % 4, 0:256]).then_inc(sSTS, 16)
                        sc.wait_ge(sSTS, sts_e(sf) - 16)
                        sc.dma_start(out=fuse_d[2 * ts2 + 1, :, kf, :],
                                     in_=e_fstg[:, sf % 4, 256:512]).then_inc(sSTS, 16)
                    for idx in range(64):
                        for f in range(8):
                            sc.wait_ge(sEVV, evv_e(idx, f, 1))
                            sf = 8 * idx + f
                            if sf >= 2:
                                i2, f2 = divmod(sf - 2, 8)
                                sc.wait_ge(sEVV, evv_e(i2, f2, 2))
                            sc.activation(e_tg2[:, f % 2], e_tg[:, f % 2],
                                          AF.Silu).then_inc(sEVS, 1)
                            if sf >= 2:
                                fstore(sf - 2)
                    fstore(510)
                    fstore(511)
                ctr["EVS"] += 8 * 64

                @block.vector
                def _(v):
                    for idx in range(64):
                        pc = idx // 8
                        for f in range(8):
                            sf = 8 * idx + f
                            c = pc * 8 + f
                            v.wait_ge(sMM, mm_e(idx, 2 * f))
                            if sf >= 2:
                                i2, f2 = divmod(sf - 2, 8)
                                v.wait_ge(sEVS, evs_e(i2, f2))
                            v.tensor_scalar(e_tg[:, f % 2], bank((2 * f) % 4),
                                            sg_sb[:, c:c + 1], None,
                                            AluOpType.mult).then_inc(sEVV, 1)
                            v.wait_ge(sMM, mm_e(idx, 2 * f + 1))
                            v.tensor_scalar(e_tu[:, f % 2], bank((2 * f + 1) % 4),
                                            su_sb[:, c:c + 1], None,
                                            AluOpType.mult).then_inc(sEVV, 1)
                            v.wait_ge(sEVV, evv_e(idx, f, 1))
                            v.wait_ge(sEVS, evs_e(idx, f))
                            if sf >= 4:
                                v.wait_ge(sSTS, sts_e(sf - 4))
                            v.tensor_tensor(e_fstg[:, sf % 4], e_tg2[:, f % 2],
                                            e_tu[:, f % 2], AluOpType.mult).then_inc(sEVV, 1)
                ctr["EVV"] += 24 * 64

            barrier()

        if upto_i >= 5:
            with (
                nc.sbuf_tensor("f_wd", [128, 64, 512], BF16) as f_wd,
                nc.sbuf_tensor("f_fu", [128, 2, 64, 256], BF16) as f_fu,
                nc.sbuf_tensor("f_sdq", [128, 512], F32) as f_sdq,
                nc.sbuf_tensor("sd_row", [1, D], F32) as sd_row,
                nc.sbuf_tensor("ones_sb", [1, 128], F32) as ones_sb,
                nc.sbuf_tensor("f_so", [128, 2, 512], F32) as f_so,
                nc.sbuf_tensor("f_x", [128, 2, 512], F32) as f_x,
                nc.sbuf_tensor("f_t1", [128, 2, 512], F32) as f_t1,
                nc.sbuf_tensor("f_t2", [128, 2, 512], F32) as f_t2,
                nc.sbuf_tensor("f_os", [128, 2, 512], F32) as f_os,
            ):
                lds0, ldg0, mm0, evv0, evs0, sts0 = (ctr["LDS"], ctr["LDG"], ctr["MM"],
                                                     ctr["EVV"], ctr["EVS"], ctr["STS"])
                def mm_q(q):
                    return mm0 + 33 * q + 1

                def mm_g(q, gg):
                    return mm0 + 33 * q + 1 + gg + 1

                def evv_q(q):
                    return evv0 + 97 * q + 1

                def evv_g(q, gg, j):
                    return evv0 + 97 * q + 1 + 3 * gg + j

                evs_g = lambda q, gg: evs0 + 32 * q + gg + 1
                ldf0 = ctr["LDF"]
                lds_fu = lambda q, blk: ldf0 + 16 * (16 * q + blk + 1)
                lds_sx = lambda q, blk, i: lds0 + 1024 * q + 64 * blk + 32 * (i + 1)
                sts_g = lambda q, gg: sts0 + 16 * (32 * q + gg + 1)

                @block.gpsimd
                def _(g):
                    g.wait_ge(sLDG, ldg0)
                    g.dma_start(out=sd_row[:], in_=s_down[None, :]).then_inc(sLDG, 16)
                    g.wait_ge(sLDG, ldg0 + 16)
                    g.dma_start(out=ones_sb[:], in_=ones_in[:]).then_inc(sLDG, 16)
                    for q in range(4):
                        if q >= 1:
                            g.wait_ge(sMM, mm_g(q - 1, 31))
                        g.wait_ge(sLDG, ldg0 + 32 + 16 * q)
                        g.dma_start(out=f_wd[:],
                                    in_=wdown_v[:, :, q * 512:(q + 1) * 512]
                                    ).then_inc(sLDG, 16)
                ctr["LDG"] += 96

                @block.sync
                def _(sy):
                    def fuse_load(fidx):
                        q3, blk3 = divmod(fidx, 16)
                        if fidx >= 2:
                            pgi = 2 * fidx - 3
                            q2, g2 = divmod(pgi, 32)
                            sy.wait_ge(sMM, mm_g(q2, g2))
                        sy.wait_ge(sLDF, ldf0 + 16 * fidx)
                        sy.dma_start(out=f_fu[:, fidx % 2],
                                     in_=fuse_d[blk3]).then_inc(sLDF, 16)

                    def out_store(gi):
                        q2, g2 = divmod(gi, 32)
                        sy.wait_ge(sEVV, evv_g(q2, g2, 3))
                        sy.wait_ge(sSTS, sts_g(q2, g2) - 16)
                        blk2, i2 = divmod(g2, 2)
                        sy.dma_start(out=out_v[:, 2 * blk2 + i2, q2 * 512:(q2 + 1) * 512],
                                     in_=f_os[:, gi % 2]).then_inc(sSTS, 16)

                    fuse_load(0)
                    for q in range(4):
                        for blk in range(16):
                            fidx = 16 * q + blk
                            if fidx + 1 < 64:
                                fuse_load(fidx + 1)
                            for i in range(2):
                                gi = 32 * q + 2 * blk + i
                                if gi >= 2:
                                    q2, g2 = divmod(gi - 2, 32)
                                    sy.wait_ge(sEVV, evv_g(q2, g2, 3))
                                sy.wait_ge(sLDS, lds_sx(q, blk, i) - 32)
                                sy.dma_start(out=f_so[:, gi % 2],
                                             in_=sout_v[:, 2 * blk + i, q * 512:(q + 1) * 512]
                                             ).then_inc(sLDS, 16)
                                sy.wait_ge(sLDS, lds_sx(q, blk, i) - 16)
                                sy.dma_start(out=f_x[:, gi % 2],
                                             in_=x_v[:, 2 * blk + i, q * 512:(q + 1) * 512]
                                             ).then_inc(sLDS, 16)
                                if gi >= 2:
                                    out_store(gi - 2)
                    out_store(126)
                    out_store(127)
                    sy.wait_ge(sSTS, sts0 + 16 * 128)
                ctr["LDS"] += 4 * 16 * 64
                ctr["LDF"] += 16 * 64
                ctr["STS"] += 16 * 128

                @block.tensor
                def _(t_):
                    t_.wait_ge(sLDG, ldg0 + 32)
                    for q in range(4):
                        t_.wait_ge(sLDG, ldg0 + 32 + 16 * (q + 1))
                        if q >= 1:
                            t_.wait_ge(sEVV, evv_q(q - 1) + 96)
                        t_.matmul(bank(4), ones_sb[:], sd_row[:, q * 512:(q + 1) * 512],
                                  start=True, stop=True).then_inc(sMM, 1)
                        for blk in range(16):
                            t_.wait_ge(sLDF, lds_fu(q, blk))
                            for i in range(2):
                                gg = 2 * blk + i
                                gi = 32 * q + gg
                                if gi >= 4:
                                    q2, g2 = divmod(gi - 4, 32)
                                    t_.wait_ge(sEVV, evv_g(q2, g2, 1))
                                ins = None
                                for k in range(64):
                                    ins = t_.matmul(bank(gg % 4),
                                                    f_fu[:, blk % 2, k, i * 128:(i + 1) * 128],
                                                    f_wd[:, k, :],
                                                    start=(k == 0), stop=(k == 63))
                                ins.then_inc(sMM, 1)
                ctr["MM"] += 33 * 4

                @block.scalar
                def _(sc):
                    for q in range(4):
                        for gg in range(32):
                            sc.wait_ge(sEVV, evv_g(q, gg, 1))
                            gi = 32 * q + gg
                            if gi >= 2:
                                q2, g2 = divmod(gi - 2, 32)
                                sc.wait_ge(sEVV, evv_g(q2, g2, 2))
                            sc.activation(f_t2[:, gg % 2], f_t1[:, gg % 2],
                                          AF.Silu).then_inc(sEVS, 1)
                ctr["EVS"] += 32 * 4

                @block.vector
                def _(v):
                    for q in range(4):
                        v.wait_ge(sMM, mm_q(q))
                        v.tensor_copy(f_sdq[:], bank(4)).then_inc(sEVV, 1)
                        v.wait_ge(sEVV, evv_q(q))
                        for blk in range(16):
                            for i in range(2):
                                gg = 2 * blk + i
                                gi = 32 * q + gg
                                v.wait_ge(sMM, mm_g(q, gg))
                                if gi >= 2:
                                    q2, g2 = divmod(gi - 2, 32)
                                    v.wait_ge(sEVS, evs_g(q2, g2))
                                v.tensor_tensor(f_t1[:, gg % 2], bank(gg % 4), f_sdq[:],
                                                AluOpType.mult).then_inc(sEVV, 1)
                                v.wait_ge(sEVS, evs_g(q, gg))
                                v.wait_ge(sLDS, lds_sx(q, blk, i))
                                v.tensor_tensor(f_t1[:, gg % 2], f_t2[:, gg % 2],
                                                f_so[:, gi % 2], AluOpType.mult).then_inc(sEVV, 1)
                                v.wait_ge(sEVV, evv_g(q, gg, 2))
                                if gi >= 2:
                                    v.wait_ge(sSTS, sts_g(*divmod(gi - 2, 32)))
                                v.tensor_tensor(f_os[:, gi % 2], f_t1[:, gg % 2],
                                                f_x[:, gi % 2], AluOpType.add).then_inc(sEVV, 1)
                ctr["EVV"] += 97 * 4

        @block.sync
        def _(sy):
            sy.wait_ge(sSTS, ctr["STS"])
            sy.wait_ge(sLDS, ctr["LDS"])

        @block.gpsimd
        def _(g):
            g.wait_ge(sLDG, ctr["LDG"])

        @block.scalar
        def _(sc):
            sc.wait_ge(sLDK, ctr["LDK"])
            sc.wait_ge(sLDV, ctr["LDV"])
            sc.wait_ge(sLDF, ctr["LDF"])

    return nc


def host_inputs(inp, core):
    import ml_dtypes as mld
    x = np.ascontiguousarray(np.asarray(inp["L0_attn_x"])[core])
    m = {}
    m["x_in"] = x
    m["xt_in"] = np.ascontiguousarray(x.T)
    m["wq_t"] = np.ascontiguousarray(np.asarray(inp["L0_attn_Wq"]).T)
    m["wk_t"] = np.ascontiguousarray(np.asarray(inp["L0_attn_Wk"]).T)
    m["wv_t"] = np.ascontiguousarray(np.asarray(inp["L0_attn_Wv"]).T)
    m["wg_t"] = np.ascontiguousarray(np.asarray(inp["L0_attn_Wg"]).T)
    m["wp_t"] = np.ascontiguousarray(np.asarray(inp["L0_attn_Wp"]).T)
    m["w_gate"] = np.ascontiguousarray(np.asarray(inp["L0_ffn_w_gate"]))
    m["w_up"] = np.ascontiguousarray(np.asarray(inp["L0_ffn_w_up"]))
    m["w_down"] = np.ascontiguousarray(np.asarray(inp["L0_ffn_w_down"]))
    m["s_gate"] = np.ascontiguousarray(np.asarray(inp["L0_ffn_s_gate"]).reshape(64, 128).T)
    m["s_up"] = np.ascontiguousarray(np.asarray(inp["L0_ffn_s_up"]).reshape(64, 128).T)
    m["s_down"] = np.ascontiguousarray(np.asarray(inp["L0_ffn_s_down"]))
    m["ident_in"] = np.eye(128, dtype=mld.bfloat16)
    m["identf_in"] = np.eye(128, dtype=np.float32)
    m["ones_in"] = np.ones((1, 128), dtype=np.float32)
    mk = np.zeros((128, 512), dtype=np.float32)
    ii = np.arange(256)
    pp = np.arange(128)[:, None]
    mk[:, 0:256] = (ii[None, :] >= pp).astype(np.float32)
    mk[:, 256:512] = (ii[None, :] >= pp + 128).astype(np.float32)
    m["masks_in"] = mk
    return m


from concourse.bass_utils import run_bass_kernel_spmd


def kernel(**inputs):
    nc = build(upto="E2", debug=False)
    in_maps = [host_inputs(inputs, core) for core in range(8)]
    res = run_bass_kernel_spmd(nc, in_maps, core_ids=list(range(8)))
    out = np.stack([np.asarray(res.results[i]["out"]) for i in range(8)], axis=0)
    return out


# version 09
# speedup vs baseline: 1.1788x; 1.0127x over previous
import contextlib
import numpy as np
import concourse.bass as bass
import concourse.mybir as mybir
from concourse.alu_op_type import AluOpType

F32 = mybir.dt.float32
BF16 = mybir.dt.bfloat16
I8 = mybir.dt.int8
AF = mybir.ActivationFunctionType
AX = mybir.AxisListType.X

D = 2048
NT = 4096
KD = 16
CH = 256
NCH = 16
INTER = 8192
LN_EPS = 1e-5
RMS_EPS = float(np.finfo(np.float32).eps)

PHASES = ["A", "B", "C", "D", "E1", "E2"]


def build(upto="E2", debug=False):
    nc = bass.Bass("TRN2", target_bir_lowering=False)
    upto_i = PHASES.index(upto)

    x_in = nc.dram_tensor("x_in", [NT, D], F32, kind="ExternalInput").ap()
    xt_in = nc.dram_tensor("xt_in", [D, NT], F32, kind="ExternalInput").ap()
    wq_t = nc.dram_tensor("wq_t", [D, D], F32, kind="ExternalInput").ap()
    wk_t = nc.dram_tensor("wk_t", [D, D], F32, kind="ExternalInput").ap()
    wv_t = nc.dram_tensor("wv_t", [D, D], F32, kind="ExternalInput").ap()
    wg_t = nc.dram_tensor("wg_t", [D, D], F32, kind="ExternalInput").ap()
    wp_t = nc.dram_tensor("wp_t", [D, D], F32, kind="ExternalInput").ap()
    w_gate = nc.dram_tensor("w_gate", [D, INTER], I8, kind="ExternalInput").ap()
    w_up = nc.dram_tensor("w_up", [D, INTER], I8, kind="ExternalInput").ap()
    w_down = nc.dram_tensor("w_down", [INTER, D], I8, kind="ExternalInput").ap()
    s_gate = nc.dram_tensor("s_gate", [128, INTER // 128], F32, kind="ExternalInput").ap()
    s_up = nc.dram_tensor("s_up", [128, INTER // 128], F32, kind="ExternalInput").ap()
    s_down = nc.dram_tensor("s_down", [D], F32, kind="ExternalInput").ap()
    ident_in = nc.dram_tensor("ident_in", [128, 128], BF16, kind="ExternalInput").ap()
    identf_in = nc.dram_tensor("identf_in", [128, 128], F32, kind="ExternalInput").ap()
    ones_in = nc.dram_tensor("ones_in", [1, 128], F32, kind="ExternalInput").ap()
    masks_in = nc.dram_tensor("masks_in", [128, 512], F32, kind="ExternalInput").ap()

    out_d = nc.dram_tensor("out", [NT, D], F32, kind="ExternalOutput").ap()

    def scratch(name, shape, dtype):
        return nc.dram_tensor(name, shape, dtype,
                              kind="ExternalOutput" if debug else "Internal").ap()

    stat_d = scratch("stat_d", [96, 128], F32)
    xnT_d = scratch("xnT_d", [D, NT], BF16)
    xn2T_d = scratch("xn2T_d", [D, NT], BF16)
    qT_d = scratch("qT_d", [D, NT], BF16)
    k_d = scratch("k_d", [NT, D], BF16)
    v_d = scratch("v_d", [NT, D], BF16)
    oT_d = scratch("oT_d", [D, NT], BF16)
    sout_d = scratch("sout_d", [NT, D], F32)
    fuse_d = scratch("fuse_d", [NT // 256, 128, INTER // 128, 256], BF16)

    xnT_v = xnT_d.rearrange("(k p) t -> p k t", p=128)
    xn2T_v = xn2T_d.rearrange("(k p) t -> p k t", p=128)
    qT_v = qT_d.rearrange("(k p) t -> p k t", p=128)
    oT_v = oT_d.rearrange("(k p) t -> p k t", p=128)
    k_v = k_d.rearrange("(j p) f -> p j f", p=128)
    v_v = v_d.rearrange("(j p) f -> p j f", p=128)
    x_v = x_in.rearrange("(j p) f -> p j f", p=128)
    xt_v = xt_in.rearrange("(k p) t -> p k t", p=128)
    out_v = out_d.rearrange("(j p) f -> p j f", p=128)
    sout_v = sout_d.rearrange("(j p) f -> p j f", p=128)
    wqt_v = wq_t.rearrange("(k p) f -> p k f", p=128)
    wkt_v = wk_t.rearrange("(k p) f -> p k f", p=128)
    wvt_v = wv_t.rearrange("(k p) f -> p k f", p=128)
    wgt_v = wg_t.rearrange("(k p) f -> p k f", p=128)
    wpt_v = wp_t.rearrange("(k p) f -> p k f", p=128)
    wgate_v = w_gate.rearrange("(k p) f -> p k f", p=128)
    wup_v = w_up.rearrange("(k p) f -> p k f", p=128)
    wdown_v = w_down.rearrange("(k p) f -> p k f", p=128)


    with contextlib.ExitStack() as _es:
        ident = _es.enter_context(nc.sbuf_tensor("ident", [128, 128], BF16))
        psum = _es.enter_context(nc.psum_tensor("psum", [128, 4096], F32))
        BAR = _es.enter_context(nc.semaphore("BAR"))
        sLDS = _es.enter_context(nc.semaphore("LDS"))
        sLDG = _es.enter_context(nc.semaphore("LDG"))
        sMM = _es.enter_context(nc.semaphore("MM"))
        sEVV = _es.enter_context(nc.semaphore("EVV"))
        sEVS = _es.enter_context(nc.semaphore("EVS"))
        sSTS = _es.enter_context(nc.semaphore("STS"))
        sLDK = _es.enter_context(nc.semaphore("LDK"))
        sLDV = _es.enter_context(nc.semaphore("LDV"))
        sLDF = _es.enter_context(nc.semaphore("LDF"))
        block = _es.enter_context(nc.Block())
        ctr = {"LDS": 0, "LDG": 0, "MM": 0, "EVV": 0, "EVS": 0, "STS": 0, "BAR": 0,
               "LDK": 0, "LDV": 0, "LDF": 0}

        def bank(b):
            return psum[:, b * 512:(b + 1) * 512]

        def region(b, off, w):
            return psum[:, b * 512 + off:b * 512 + off + w]

        def barrier():
            ctr["BAR"] += 5
            tgt = ctr["BAR"]
            snap = dict(ctr)

            def drain(eng, name):
                eng.wait_ge(sMM, snap["MM"])
                eng.wait_ge(sEVV, snap["EVV"])
                eng.wait_ge(sEVS, snap["EVS"])
                eng.wait_ge(sLDS, snap["LDS"])
                eng.wait_ge(sLDK, snap["LDK"])
                eng.wait_ge(sLDV, snap["LDV"])
                eng.wait_ge(sLDF, snap["LDF"])
                eng.wait_ge(sSTS, snap["STS"])
                eng.wait_ge(sLDG, snap["LDG"])

            for name in ["tensor", "vector", "scalar", "gpsimd", "sync"]:
                def mk(eng_name):
                    def body(eng):
                        drain(eng, eng_name)
                        eng.nop().then_inc(BAR, 1)
                        eng.wait_ge(BAR, tgt)
                    return body
                getattr(block, name)(mk(name))

        @block.sync
        def _(sy):
            sy.dma_start(out=ident[:], in_=ident_in[:]).then_inc(sLDS, 16)
        ctr["LDS"] += 16

        @block.tensor
        def _(t):
            t.wait_ge(sLDS, ctr["LDS"])

        _ab_es = contextlib.ExitStack()
        bw0 = _ab_es.enter_context(nc.sbuf_tensor("bw0", [128, KD, D], BF16))
        b_x = _ab_es.enter_context(nc.sbuf_tensor("b_x", [128, 2, KD, 512], BF16))
        b_qstg = _ab_es.enter_context(nc.sbuf_tensor("b_qstg", [128, 2, 512], BF16))

        _a_es = contextlib.ExitStack()
        a_bst = _a_es.enter_context(nc.sbuf_tensor("a_bst", [128, 3, NT], F32))
        a_srow = _a_es.enter_context(nc.sbuf_tensor("a_srow", [1, 2, 512], F32))
        a_ones = _a_es.enter_context(nc.sbuf_tensor("a_ones", [1, 128], F32))
        a_idf = _a_es.enter_context(nc.sbuf_tensor("a_idf", [128, 128], F32))
        a_st = _a_es.enter_context(nc.sbuf_tensor("a_st", [128, 12, 32], F32))
        a_pk = _a_es.enter_context(nc.sbuf_tensor("a_pk", [128, 96], F32))
        a_stt = _a_es.enter_context(nc.sbuf_tensor("a_stt", [96, 128], F32))
        _a1_es = contextlib.ExitStack()
        a_xb = _a1_es.enter_context(nc.sbuf_tensor("a_xb", [128, 2, 2, D], F32))
        a_sq = _a1_es.enter_context(nc.sbuf_tensor("a_sq", [128, 2, D], F32))
        _a1_es.close()
        a_xt = _a_es.enter_context(nc.sbuf_tensor("a_xt", [128, 2, 2048], F32))
        a_tmp = _a_es.enter_context(nc.sbuf_tensor("a_tmp", [128, 2, 2048], F32))
        a_xnb = _a_es.enter_context(nc.sbuf_tensor("a_xnb", [128, 2, 2048], BF16))
        a_x2b = _a_es.enter_context(nc.sbuf_tensor("a_x2b", [128, 2, 2048], BF16))
        if True:
            NTILE = 32
            lds0, ldg0, mm0, evv0, evs0, sts0, ldv0 = (ctr["LDS"], ctr["LDG"], ctr["MM"],
                                                       ctr["EVV"], ctr["EVS"], ctr["STS"],
                                                       ctr["LDV"])
            lds_p1 = lambda li: lds0 + 16 * (2 + li + 1)
            lds_sr = lambda j: lds0 + 16 * (18 + j + 1)
            lds_p2 = lambda i: lds0 + 16 * (42 + i + 1)
            evv_red = lambda ti: evv0 + ti + 1
            evv_ch = lambda o: evv0 + 32 + o
            evv_sub = lambda i: evv0 + 47 + 2 * i + 1
            evv_mul = lambda i: evv0 + 47 + 2 * i + 2
            evs_sq = lambda ti: evs0 + ti + 1
            evs_bc = lambda j: evs0 + 34 + j + 1
            mm_bc = lambda j: mm0 + j + 2
            sts_xn = lambda i: sts0 + 16 * (2 * i + 2)
            sts_x2 = lambda i: sts0 + 16 * (2 * i + 3)
            ldv_tt = lambda i: ldv0 + i + 1

            @block.sync
            def _(sy):
                sy.wait_ge(sLDS, lds0)
                sy.dma_start(out=a_idf[:], in_=identf_in[:]).then_inc(sLDS, 16)
                sy.wait_ge(sLDS, lds0 + 16)
                sy.dma_start(out=a_ones[:], in_=ones_in[:]).then_inc(sLDS, 16)
                for li in range(16):
                    if li >= 2:
                        sy.wait_ge(sEVV, evv_red(2 * (li - 2) + 1))
                        sy.wait_ge(sEVS, evs_sq(2 * (li - 2) + 1))
                    sy.wait_ge(sLDS, lds_p1(li) - 16)
                    sy.dma_start(out=a_xb[:, li % 2],
                                 in_=x_v[:, 2 * li:2 * li + 2, :]).then_inc(sLDS, 16)
                flat = stat_d.rearrange("a p -> (a p)")
                for j in range(24):
                    s3, ts = divmod(j, 8)
                    if j < 2:
                        sy.wait_ge(sSTS, sts0 + 16)
                    else:
                        sy.wait_ge(sMM, mm_bc(j - 2))
                    sy.wait_ge(sLDS, lds_sr(j) - 16)
                    sy.dma_start(out=a_srow[:, j % 2, :],
                                 in_=flat[None, s3 * 4096 + ts * 512:
                                          s3 * 4096 + ts * 512 + 512]).then_inc(sLDS, 16)
                for i in range(NTILE):
                    ts2, k = divmod(i, 16)
                    if i == 0:
                        sy.wait_ge(sEVS, evs_bc(23))
                    if i >= 2:
                        sy.wait_ge(sEVV, evv_sub(i - 2))
                    sy.wait_ge(sLDS, lds_p2(i) - 16)
                    sy.dma_start(out=a_xt[:, i % 2],
                                 in_=xt_v[:, k, ts2 * 2048:(ts2 + 1) * 2048]
                                 ).then_inc(sLDS, 16)
            ctr["LDS"] += 16 * 74

            @block.gpsimd
            def _(g):
                g.wait_ge(sLDG, ldg0)
                g.dma_start(out=bw0[:], in_=wqt_v[:]).then_inc(sLDG, 16)
                for i in range(NTILE):
                    g.wait_ge(sEVV, evv_sub(i))
                    if i >= 2:
                        g.wait_ge(sSTS, sts_x2(i - 2))
                    ts2, k = divmod(i, 16)
                    g.tensor_tensor(a_x2b[:, i % 2], a_tmp[:, i % 2],
                                    a_bst[:, 2, ts2 * 2048:(ts2 + 1) * 2048],
                                    AluOpType.mult).then_inc(sLDV, 1)
            ctr["LDG"] += 16
            ctr["LDV"] += NTILE

            @block.scalar
            def _(sc):
                for ti in range(NTILE):
                    sc.wait_ge(sLDS, lds_p1(ti // 2))
                    if ti >= 2:
                        sc.wait_ge(sEVS, evs_sq(ti - 2))
                    sc.activation(a_sq[:, ti % 2], a_xb[:, (ti // 2) % 2, ti % 2, :],
                                  AF.Square,
                                  accum_out=a_st[:, 1, ti:ti + 1]).then_inc(sEVS, 1)
                sc.wait_ge(sEVV, evv_ch(5))
                sc.activation(a_st[:, 6], a_st[:, 5], AF.Sqrt).then_inc(sEVS, 1)
                sc.wait_ge(sEVV, evv_ch(9))
                sc.activation(a_st[:, 9], a_st[:, 8], AF.Sqrt).then_inc(sEVS, 1)
                sc.wait_ge(sEVV, evv_ch(15))
                sc.wait_ge(sSTS, sts0)
                sc.dma_start(out=stat_d[:], in_=a_stt[:]).then_inc(sSTS, 16)
                for j in range(24):
                    s3, ts = divmod(j, 8)
                    sc.wait_ge(sMM, mm_bc(j))
                    sc.activation(a_bst[:, s3, ts * 512:(ts + 1) * 512],
                                  bank(j % 8), AF.Copy).then_inc(sEVS, 1)
                for i in range(NTILE):
                    ts2, k = divmod(i, 16)
                    sc.wait_ge(sEVV, evv_mul(i))
                    sc.wait_ge(sSTS, sts_xn(i) - 16)
                    sc.dma_start(out=xnT_v[:, k, ts2 * 2048:(ts2 + 1) * 2048],
                                 in_=a_xnb[:, i % 2]).then_inc(sSTS, 16)
                    sc.wait_ge(sLDV, ldv_tt(i))
                    sc.wait_ge(sSTS, sts_x2(i) - 16)
                    sc.dma_start(out=xn2T_v[:, k, ts2 * 2048:(ts2 + 1) * 2048],
                                 in_=a_x2b[:, i % 2]).then_inc(sSTS, 16)
            ctr["EVS"] += 58
            ctr["STS"] += 16 * 65

            @block.vector
            def _(v):
                st = a_st
                for li in range(16):
                    v.wait_ge(sLDS, lds_p1(li))
                    for jj in range(2):
                        v.reduce_sum(st[:, 0, 2 * li + jj:2 * li + jj + 1],
                                     a_xb[:, li % 2, jj, :], axis=AX).then_inc(sEVV, 1)
                v.wait_ge(sEVV, evv0 + NTILE)
                o = 0
                def step(ins):
                    nonlocal o
                    o += 1
                    ins.then_inc(sEVV, 1)
                    v.wait_ge(sEVV, evv_ch(o))
                step(v.tensor_scalar(st[:, 2], st[:, 0], 1.0 / D, None, AluOpType.mult))
                v.wait_ge(sEVS, evs0 + NTILE)
                step(v.tensor_scalar(st[:, 3], st[:, 1], 1.0 / D, None, AluOpType.mult))
                step(v.tensor_tensor(st[:, 4], st[:, 2], st[:, 2], AluOpType.mult))
                step(v.tensor_tensor(st[:, 4], st[:, 3], st[:, 4], AluOpType.subtract))
                step(v.tensor_scalar(st[:, 5], st[:, 4], LN_EPS, None, AluOpType.add))
                v.wait_ge(sEVS, evs0 + NTILE + 1)
                step(v.reciprocal(st[:, 7], st[:, 6]))
                step(v.reciprocal(st[:, 8], st[:, 5]))
                step(v.tensor_tensor(st[:, 8], st[:, 4], st[:, 8], AluOpType.mult))
                step(v.tensor_scalar(st[:, 8], st[:, 8], RMS_EPS, None, AluOpType.add))
                v.wait_ge(sEVS, evs0 + NTILE + 2)
                step(v.reciprocal(st[:, 10], st[:, 9]))
                step(v.tensor_tensor(st[:, 11], st[:, 7], st[:, 10], AluOpType.mult))
                assert o == 11
                step(v.tensor_copy(a_pk[:, 0:32], st[:, 2]))
                step(v.tensor_copy(a_pk[:, 32:64], st[:, 7]))
                step(v.tensor_copy(a_pk[:, 64:96], st[:, 11]))
                v.wait_ge(sMM, mm0 + 1)
                v.tensor_copy(a_stt[:], psum[0:96, 0:128]).then_inc(sEVV, 1)
                v.wait_ge(sEVV, evv_ch(15))
                for i in range(NTILE):
                    ts2, k = divmod(i, 16)
                    v.wait_ge(sLDS, lds_p2(i))
                    if i >= 2:
                        v.wait_ge(sLDV, ldv_tt(i - 2))
                        v.wait_ge(sSTS, sts_xn(i - 2))
                    v.tensor_tensor(a_tmp[:, i % 2], a_xt[:, i % 2],
                                    a_bst[:, 0, ts2 * 2048:(ts2 + 1) * 2048],
                                    AluOpType.subtract).then_inc(sEVV, 1)
                    v.wait_ge(sEVV, evv_sub(i))
                    v.tensor_tensor(a_xnb[:, i % 2], a_tmp[:, i % 2],
                                    a_bst[:, 1, ts2 * 2048:(ts2 + 1) * 2048],
                                    AluOpType.mult).then_inc(sEVV, 1)
            ctr["EVV"] += 47 + 2 * NTILE

            @block.tensor
            def _(t):
                t.wait_ge(sEVV, evv_ch(14))
                t.wait_ge(sLDS, lds0 + 16)
                t.matmul(psum[0:96, 0:128], a_pk[:], a_idf[:],
                         start=True, stop=True).then_inc(sMM, 1)
                for j in range(24):
                    t.wait_ge(sLDS, lds_sr(j))
                    if j == 0:
                        t.wait_ge(sEVV, evv_ch(15))
                        t.wait_ge(sLDS, lds0 + 32)
                    if j >= 8:
                        t.wait_ge(sEVS, evs_bc(j - 8))
                    t.matmul(bank(j % 8), a_ones[:], a_srow[:, j % 2, :],
                             start=True, stop=True).then_inc(sMM, 1)
            ctr["MM"] += 25
            a_ldg_wq = ldg0 + 16
            a_sts_half = [sts_x2(15), sts_x2(31)]
            a_sts_last = sts_x2(31)
            a_evs_bc_vals = [evs_bc(16 + b) for b in range(4)]

        _a_es.close()

        if upto_i >= 1:
            with (
                nc.sbuf_tensor("b_w1", [128, KD, D], BF16) as b_w1,
                nc.sbuf_tensor("b_kstg", [128, 2, D], BF16) as b_kstg,
            ):
                b_bufs = [bw0, b_w1, bw0]
                lds0, ldg0, mm0, evs0, sts0 = (ctr["LDS"], ctr["LDG"], ctr["MM"],
                                               ctr["EVS"], ctr["STS"])
                mm_g = lambda p, ts, g: mm0 + 128 * p + 16 * ts + g + 1
                evs_g = lambda p, ts, g: evs0 + 128 * p + 16 * ts + g + 1

                sts_sched = []
                stv = sts0
                store_of_stream = {}
                for p in range(3):
                    for ts in range(8):
                        if p == 0:
                            for f in range(16):
                                stv += 16
                                store_of_stream[("q", ts, f)] = stv
                        else:
                            for ib in range(4):
                                stv += 16
                                store_of_stream[("kv", p, ts, ib)] = stv

                @block.gpsimd
                def _(g):
                    g.wait_ge(sSTS, a_sts_last)
                    g.wait_ge(sLDG, ldg0)
                    g.dma_start(out=b_w1[:], in_=wkt_v[:]).then_inc(sLDG, 16)
                    g.wait_ge(sMM, mm_g(0, 7, 15))
                    g.wait_ge(sLDG, ldg0 + 16)
                    g.dma_start(out=bw0[:], in_=wvt_v[:]).then_inc(sLDG, 16)
                ctr["LDG"] += 32

                @block.sync
                def _(sy):
                    for p in range(3):
                        for ts in range(8):
                            idx = 8 * p + ts
                            if idx < 8:
                                sy.wait_ge(sSTS, a_sts_half[0 if ts < 4 else 1])
                            if idx >= 2:
                                sy.wait_ge(sMM, mm0 + 16 * (idx - 1))
                            sy.wait_ge(sLDS, lds0 + 16 * idx)
                            sy.dma_start(out=b_x[:, idx % 2],
                                         in_=xn2T_v[:, :, ts * 512:(ts + 1) * 512]
                                         ).then_inc(sLDS, 16)
                ctr["LDS"] += 16 * 24

                @block.tensor
                def _(t):
                    t.wait_ge(sLDG, a_ldg_wq)
                    for b4 in range(4):
                        t.wait_ge(sEVS, a_evs_bc_vals[b4])
                    for p in range(3):
                        wbuf = b_bufs[p][:]
                        if p > 0:
                            t.wait_ge(sLDG, ldg0 + 16 * p)
                        for ts in range(8):
                            idx = 8 * p + ts
                            t.wait_ge(sLDS, lds0 + 16 * (idx + 1))
                            xbuf = b_x[:, idx % 2]
                            for g in range(16):
                                gi = 128 * p + 16 * ts + g
                                if gi >= 4:
                                    pi, r = divmod(gi - 4, 128)
                                    ti, g2 = divmod(r, 16)
                                    t.wait_ge(sEVS, evs_g(pi, ti, g2))
                                ins = None
                                if p == 0:
                                    for k in range(KD):
                                        ins = t.matmul(bank(g % 4),
                                                       wbuf[:, k, g * 128:(g + 1) * 128],
                                                       xbuf[:, k, :],
                                                       start=(k == 0), stop=(k == KD - 1))
                                else:
                                    ib, ot = divmod(g, 4)
                                    for k in range(KD):
                                        ins = t.matmul(bank(g % 4),
                                                       xbuf[:, k, ib * 128:(ib + 1) * 128],
                                                       wbuf[:, k, ot * 512:(ot + 1) * 512],
                                                       start=(k == 0), stop=(k == KD - 1))
                                ins.then_inc(sMM, 1)
                ctr["MM"] += 3 * 128

                @block.scalar
                def _(sc):
                    for p in range(3):
                        for ts in range(8):
                            for g in range(16):
                                sc.wait_ge(sMM, mm_g(p, ts, g))
                                if p == 0:
                                    sf = 16 * ts + g
                                    if sf >= 2:
                                        sc.wait_ge(sSTS, store_of_stream[("q", (sf - 2) // 16,
                                                                          (sf - 2) % 16)])
                                    sc.activation(b_qstg[:, sf % 2], bank(g % 4),
                                                  AF.Silu).then_inc(sEVS, 1)
                                    sc.wait_ge(sEVS, evs_g(p, ts, g))
                                    sc.wait_ge(sSTS, store_of_stream[("q", ts, g)] - 16)
                                    sc.dma_start(out=qT_v[:, g, ts * 512:(ts + 1) * 512],
                                                 in_=b_qstg[:, sf % 2]).then_inc(sSTS, 16)
                                else:
                                    ib, ot = divmod(g, 4)
                                    ks = (p - 1) * 32 + ts * 4 + ib
                                    if ot == 0 and ks >= 2:
                                        k2 = ks - 2
                                        p2 = k2 // 32 + 1
                                        t2, i2 = divmod(k2 % 32, 4)
                                        sc.wait_ge(sSTS, store_of_stream[("kv", p2, t2, i2)])
                                    sc.activation(b_kstg[:, ks % 2, ot * 512:(ot + 1) * 512],
                                                  bank(g % 4), AF.Silu).then_inc(sEVS, 1)
                                    if ot == 3:
                                        sc.wait_ge(sEVS, evs_g(p, ts, g))
                                        sc.wait_ge(sSTS, store_of_stream[("kv", p, ts, ib)] - 16)
                                        dst = k_v if p == 1 else v_v
                                        sc.dma_start(out=dst[:, ts * 4 + ib, :],
                                                     in_=b_kstg[:, ks % 2]).then_inc(sSTS, 16)
                ctr["EVS"] += 3 * 128
                ctr["STS"] = stv

            barrier()
        _ab_es.close()

        if upto_i >= 2:
            with (
                nc.sbuf_tensor("c_kv", [128, KD, D], F32) as c_kv,
                nc.sbuf_tensor("c_q", [128, 2, KD, CH], BF16) as c_q,
                nc.sbuf_tensor("c_k", [128, 2, 2, D], BF16) as c_k,
                nc.sbuf_tensor("c_v", [128, 2, D], BF16) as c_v,
                nc.sbuf_tensor("c_kT", [128, KD, CH], BF16) as c_kT,
                nc.sbuf_tensor("c_sT", [128, 2, CH], BF16) as c_sT,
                nc.sbuf_tensor("c_oT", [128, 2, KD, CH], BF16) as c_oT,
                nc.sbuf_tensor("c_kvbf", [128, 4, 1024], BF16) as c_kvbf,
                nc.sbuf_tensor("masks", [128, 512], F32) as masks,
            ):
                lds0, mm0, evv0, evs0, sts0 = (ctr["LDS"], ctr["MM"], ctr["EVV"],
                                               ctr["EVS"], ctr["STS"])
                def mm_base(t):
                    return mm0 + (0 if t == 0 else 42 + 74 * (t - 1))

                def mm_end(t):
                    return mm0 + 42 + 74 * t if t >= 0 else mm0

                def mm_a(t, g):
                    return mm_base(t) + g + 1

                def mm_b(t, j):
                    return mm_base(t) + 8 + j + 1

                def mm_ik(t, h, k):
                    return mm_base(t) + 10 + 24 * h + k + 1

                def mm_ie(t, h, e):
                    if t == 0:
                        return mm_base(t) + 10 + 8 * h + e + 1
                    return mm_base(t) + 10 + 24 * h + 16 + e + 1

                def mm_d(t, k):
                    if t == 0:
                        return mm_base(t) + 26 + k + 1
                    return mm_base(t) + 58 + k + 1

                evv_c = lambda t, o: evv0 + 42 * t + o
                evs_stage = lambda t, h, k: evs0 + 32 * (t - 1) + 16 * h + k + 1
                ldk0, ldv0 = ctr["LDK"], ctr["LDV"]
                sts_o = lambda t: sts0 + 16 * (t + 1)

                @block.sync
                def _(sy):
                    for t in range(NCH):
                        if t >= 2:
                            sy.wait_ge(sMM, mm_end(t - 2))
                        sy.wait_ge(sLDS, lds0 + 16 * t)
                        sy.dma_start(out=c_q[:, t % 2],
                                     in_=qT_v[:, :, t * CH:(t + 1) * CH]).then_inc(sLDS, 16)
                        sy.wait_ge(sLDK, ldk0 + 16 * t)
                        sy.dma_start(out=c_k[:, t % 2],
                                     in_=k_v[:, 2 * t:2 * t + 2, :]).then_inc(sLDK, 16)
                        if t >= 1:
                            sy.wait_ge(sMM, mm_end(t - 1))
                        sy.wait_ge(sLDV, ldv0 + 16 * t)
                        sy.dma_start(out=c_v[:],
                                     in_=v_v[:, 2 * t:2 * t + 2, :]).then_inc(sLDV, 16)
                        if t >= 1:
                            sy.wait_ge(sEVV, evv_c(t - 1, 26))
                            sy.wait_ge(sSTS, sts_o(t - 1) - 16)
                            sy.dma_start(out=oT_v[:, :, (t - 1) * CH:t * CH],
                                         in_=c_oT[:, (t - 1) % 2]).then_inc(sSTS, 16)
                    sy.wait_ge(sEVV, evv_c(NCH - 1, 26))
                    sy.wait_ge(sSTS, sts_o(NCH - 1) - 16)
                    sy.dma_start(out=oT_v[:, :, (NCH - 1) * CH:NCH * CH],
                                 in_=c_oT[:, (NCH - 1) % 2]).then_inc(sSTS, 16)
                ctr["LDS"] += 16 * NCH
                ctr["LDK"] += 16 * NCH
                ctr["LDV"] += 16 * NCH
                ctr["STS"] += 16 * NCH

                ldgC0 = ctr["LDG"]

                @block.gpsimd
                def _(g):
                    g.wait_ge(sLDG, ldgC0)
                    g.dma_start(out=masks[:], in_=masks_in[:]).then_inc(sLDG, 16)
                ctr["LDG"] += 16

                @block.scalar
                def _(sc):
                    for t in range(1, NCH):
                        for h in range(2):
                            for k in range(KD):
                                idx = 16 * h + k
                                sc.wait_ge(sEVV, evv_c(t - 1, 26 + k + 1))
                                if idx >= 4:
                                    h2, k2 = divmod(idx - 4, 16)
                                    sc.wait_ge(sMM, mm_ik(t, h2, k2))
                                elif t >= 2:
                                    h2, k2 = divmod(idx - 4 + 32, 16)
                                    sc.wait_ge(sMM, mm_ik(t - 1, h2, k2))
                                sc.activation(c_kvbf[:, idx % 4],
                                              c_kv[:, k, h * 1024:(h + 1) * 1024],
                                              AF.Copy).then_inc(sEVS, 1)
                ctr["EVS"] += 32 * (NCH - 1)

                @block.tensor
                def _(t_):
                    for t in range(NCH):
                        t_.wait_ge(sLDS, lds0 + 16 * (t + 1))
                        t_.wait_ge(sLDK, ldk0 + 16 * (t + 1))
                        for g in range(8):
                            if t >= 1:
                                t_.wait_ge(sEVV, evv_c(t - 1, 41 if g < 4 else 42))
                            j, ks = divmod(4 * g, 16)
                            ins = None
                            for kk in range(4):
                                k = ks + kk
                                ins = t_.matmul(region(g, kk * 128, 128),
                                                c_k[:, t % 2, j, k * 128:(k + 1) * 128],
                                                ident[:], start=True, stop=True)
                            ins.then_inc(sMM, 1)
                        for j in range(2):
                            t_.wait_ge(sEVV, evv_c(t, 4 * (j + 1)))
                            ins = None
                            for k in range(KD):
                                ins = t_.matmul(region(j, 0, CH),
                                                c_kT[:, k, j * 128:(j + 1) * 128],
                                                c_q[:, t % 2, k, :],
                                                start=(k == 0), stop=(k == KD - 1))
                            ins.then_inc(sMM, 1)
                        t_.wait_ge(sEVV, evv_c(t, 10))
                        t_.wait_ge(sLDV, ldv0 + 16 * (t + 1))
                        for h in range(2):
                            if h == 1:
                                t_.wait_ge(sEVV, evv_c(t, 18))
                            if t >= 1:
                                for k in range(KD):
                                    idx = 16 * h + k
                                    t_.wait_ge(sEVS, evs_stage(t, h, k))
                                    for b in range(8):
                                        ins = t_.matmul(
                                            region(b, 0, CH),
                                            c_kvbf[:, idx % 4, b * 128:(b + 1) * 128],
                                            c_q[:, t % 2, k, :],
                                            start=(k == 0), stop=False,
                                            skip_group_check=True)
                                    ins.then_inc(sMM, 1)
                            for b in range(8):
                                e = 8 * h + b
                                ins = None
                                for j in range(2):
                                    ins = t_.matmul(region(b, 0, CH),
                                                    c_v[:, j, e * 128:(e + 1) * 128],
                                                    c_sT[:, j, :],
                                                    start=(t == 0 and j == 0),
                                                    stop=(j == 1),
                                                    skip_group_check=True)
                                ins.then_inc(sMM, 1)
                        t_.wait_ge(sEVV, evv_c(t, 26))
                        for k in range(KD):
                            gk = k % 2
                            if k >= 2:
                                t_.wait_ge(sEVV, evv_c(t, 26 + k - 1))
                            ins = None
                            for et in range(4):
                                for j in range(2):
                                    ins = t_.matmul(bank(gk * 4 + et),
                                                    c_k[:, t % 2, j, k * 128:(k + 1) * 128],
                                                    c_v[:, j, et * 512:(et + 1) * 512],
                                                    start=(j == 0), stop=(j == 1),
                                                    skip_group_check=True)
                            ins.then_inc(sMM, 1)
                ctr["MM"] += 42 + 74 * (NCH - 1)

                @block.vector
                def _(v):
                    v.wait_ge(sLDG, ldgC0 + 16)
                    for t in range(NCH):
                        for g in range(8):
                            v.wait_ge(sMM, mm_a(t, g))
                            j, ks = divmod(4 * g, 16)
                            v.tensor_copy(c_kT[:, ks:ks + 4, j * 128:(j + 1) * 128],
                                          bank(g).rearrange("p (a c) -> p a c", a=4)
                                          ).then_inc(sEVV, 1)
                        for j in range(2):
                            v.wait_ge(sMM, mm_b(t, j))
                            v.tensor_tensor(c_sT[:, j, :], region(j, 0, CH),
                                            masks[:, j * 256:(j + 1) * 256],
                                            AluOpType.mult).then_inc(sEVV, 1)
                        if t >= 2:
                            v.wait_ge(sSTS, sts_o(t - 2))
                        for h in range(2):
                            for b in range(8):
                                e = 8 * h + b
                                v.wait_ge(sMM, mm_ie(t, h, b))
                                v.tensor_copy(c_oT[:, t % 2, e, :],
                                              region(b, 0, CH)).then_inc(sEVV, 1)
                        for k in range(KD):
                            gk = k % 2
                            v.wait_ge(sMM, mm_d(t, k))
                            if t >= 1:
                                v.wait_ge(sEVS, evs_stage(t, 1, k))
                            span = psum[:, gk * 2048:(gk + 1) * 2048]
                            dst = c_kv[:, k, :]
                            if t == 0:
                                ins = v.tensor_copy(dst, span)
                            else:
                                ins = v.tensor_tensor(dst, span, dst, AluOpType.add)
                            ins.then_inc(sEVV, 1)
                ctr["EVV"] += 42 * NCH

            barrier()

        if upto_i >= 3:
            with (
                nc.sbuf_tensor("d_wg", [128, KD, D], BF16) as d_wg,
                nc.sbuf_tensor("d_wp", [128, KD, D], BF16) as d_wp,
                nc.sbuf_tensor("d_oT", [128, 2, KD, 256], BF16) as d_oT,
                nc.sbuf_tensor("d_x2", [128, 2, KD, 256], BF16) as d_x2,
                nc.sbuf_tensor("d_og", [128, 2, KD, 256], BF16) as d_og,
                nc.sbuf_tensor("d_sig", [128, 2, 256], F32) as d_sig,
                nc.sbuf_tensor("d_sstg", [128, 2, D], F32) as d_sstg,
            ):
                lds0, ldg0, mm0, evv0, evs0, sts0 = (ctr["LDS"], ctr["LDG"], ctr["MM"],
                                                     ctr["EVV"], ctr["EVS"], ctr["STS"])
                NSL = 16
                mm_pg = lambda t, g: mm0 + 24 * t + g + 1
                mm_ps = lambda t, s: mm0 + 24 * t + 16 + s + 1
                evs_sg = lambda t, g: evs0 + 16 * t + g + 1
                evv_og = lambda t, g: evv0 + 24 * t + g + 1
                evv_ev = lambda t, s: evv0 + 24 * t + 16 + s + 1
                sts_d = lambda t, i: sts0 + 32 * t + 16 * (i + 1)

                @block.gpsimd
                def _(g):
                    g.wait_ge(sLDG, ldg0)
                    g.dma_start(out=d_wg[:, :, 0:1024], in_=wgt_v[:, :, 0:1024]).then_inc(sLDG, 16)
                    g.wait_ge(sLDG, ldg0 + 16)
                    g.dma_start(out=d_wg[:, :, 1024:2048], in_=wgt_v[:, :, 1024:2048]).then_inc(sLDG, 16)
                    g.wait_ge(sLDG, ldg0 + 32)
                    g.dma_start(out=d_wp[:], in_=wpt_v[:]).then_inc(sLDG, 16)
                ctr["LDG"] += 48

                @block.sync
                def _(sy):
                    for t in range(NSL):
                        if t >= 2:
                            sy.wait_ge(sEVV, evv_og(t - 2, 15))
                            sy.wait_ge(sMM, mm_pg(t - 2, 15))
                        sy.wait_ge(sLDS, lds0 + 32 * t)
                        sy.dma_start(out=d_oT[:, t % 2],
                                     in_=oT_v[:, :, t * 256:(t + 1) * 256]).then_inc(sLDS, 16)
                        sy.wait_ge(sLDS, lds0 + 32 * t + 16)
                        sy.dma_start(out=d_x2[:, t % 2],
                                     in_=xn2T_v[:, :, t * 256:(t + 1) * 256]).then_inc(sLDS, 16)
                        if t >= 1:
                            for i in range(2):
                                sy.wait_ge(sEVV, evv_ev(t - 1, 4 * i + 3))
                                sy.wait_ge(sSTS, sts_d(t - 1, i) - 16)
                                sy.dma_start(out=sout_v[:, (t - 1) * 2 + i, :],
                                             in_=d_sstg[:, i]).then_inc(sSTS, 16)
                    for i in range(2):
                        sy.wait_ge(sEVV, evv_ev(NSL - 1, 4 * i + 3))
                        sy.wait_ge(sSTS, sts_d(NSL - 1, i) - 16)
                        sy.dma_start(out=sout_v[:, (NSL - 1) * 2 + i, :],
                                     in_=d_sstg[:, i]).then_inc(sSTS, 16)
                ctr["LDS"] += 32 * NSL
                ctr["STS"] += 32 * NSL

                @block.tensor
                def _(t_):
                    t_.wait_ge(sLDG, ldg0 + 16)
                    for t in range(NSL):
                        t_.wait_ge(sLDS, lds0 + 32 * t + 32)
                        if t == 0:
                            pass
                        for g in range(16):
                            if t == 0 and g == 8:
                                t_.wait_ge(sLDG, ldg0 + 32)
                            b = g % 8
                            if g < 8:
                                if t >= 1:
                                    t_.wait_ge(sEVV, evv_ev(t - 1, b + 1))
                            else:
                                t_.wait_ge(sEVS, evs_sg(t, g - 8))
                            ins = None
                            for k in range(KD):
                                ins = t_.matmul(region(b, 0, 256),
                                                d_wg[:, k, g * 128:(g + 1) * 128],
                                                d_x2[:, t % 2, k, :],
                                                start=(k == 0), stop=(k == KD - 1))
                            ins.then_inc(sMM, 1)
                        if t == 0:
                            t_.wait_ge(sLDG, ldg0 + 48)
                        t_.wait_ge(sEVV, evv_og(t, 15))
                        for s in range(8):
                            i, ot = divmod(s, 4)
                            b = s % 8
                            t_.wait_ge(sEVS, evs_sg(t, 8 + b))
                            ins = None
                            for e in range(KD):
                                ins = t_.matmul(bank(b),
                                                d_og[:, t % 2, e, i * 128:(i + 1) * 128],
                                                d_wp[:, e, ot * 512:(ot + 1) * 512],
                                                start=(e == 0), stop=(e == KD - 1))
                            ins.then_inc(sMM, 1)
                ctr["MM"] += 24 * NSL

                @block.scalar
                def _(sc):
                    for t in range(NSL):
                        for g in range(16):
                            sc.wait_ge(sMM, mm_pg(t, g))
                            if g >= 2 or t >= 1:
                                g2 = g - 2 if g >= 2 else g + 14
                                t2 = t if g >= 2 else t - 1
                                sc.wait_ge(sEVV, evv_og(t2, g2))
                            sc.activation(d_sig[:, g % 2], region(g % 8, 0, 256),
                                          AF.Sigmoid).then_inc(sEVS, 1)
                ctr["EVS"] += 16 * NSL

                @block.vector
                def _(v):
                    for t in range(NSL):
                        if t >= 2:
                            v.wait_ge(sMM, mm_ps(t - 2, 8))
                        for g in range(16):
                            v.wait_ge(sEVS, evs_sg(t, g))
                            v.tensor_tensor(d_og[:, t % 2, g, :], d_oT[:, t % 2, g, :],
                                            d_sig[:, g % 2], AluOpType.mult).then_inc(sEVV, 1)
                        for s in range(8):
                            i, ot = divmod(s, 4)
                            v.wait_ge(sMM, mm_ps(t, s))
                            if t >= 1 and s % 4 == 0:
                                v.wait_ge(sSTS, sts_d(t - 1, i))
                            v.tensor_copy(d_sstg[:, i, ot * 512:(ot + 1) * 512],
                                          bank(s % 8)).then_inc(sEVV, 1)
                ctr["EVV"] += 24 * NSL

            barrier()

        if upto_i >= 4:
            with (
                nc.sbuf_tensor("e_wg", [128, 2, KD, 1024], BF16) as e_wg,
                nc.sbuf_tensor("e_wu", [128, 2, KD, 1024], BF16) as e_wu,
                nc.sbuf_tensor("e_x", [128, 2, KD, 512], BF16) as e_x,
                nc.sbuf_tensor("e_tg", [128, 2, 512], F32) as e_tg,
                nc.sbuf_tensor("e_tg2", [128, 2, 512], F32) as e_tg2,
                nc.sbuf_tensor("e_tu", [128, 2, 512], F32) as e_tu,
                nc.sbuf_tensor("e_fstg", [128, 4, 512], BF16) as e_fstg,
                nc.sbuf_tensor("sg_sb", [128, 64], F32) as sg_sb,
                nc.sbuf_tensor("su_sb", [128, 64], F32) as su_sb,
            ):
                lds0, ldg0, mm0, evv0, evs0, sts0 = (ctr["LDS"], ctr["LDG"], ctr["MM"],
                                                     ctr["EVV"], ctr["EVS"], ctr["STS"])
                NP = 8
                mm_e = lambda idx, g: mm0 + 16 * idx + g + 1
                evv_e = lambda idx, f, j: evv0 + 24 * idx + 3 * f + j + 1
                evs_e = lambda idx, f: evs0 + 8 * idx + f + 1
                sts_e = lambda sf: sts0 + 32 * (sf + 1)
                ldg_e = lambda pc: ldg0 + 32 + 32 * (pc + 1)

                @block.gpsimd
                def _(g):
                    g.wait_ge(sLDG, ldg0)
                    g.dma_start(out=sg_sb[:], in_=s_gate[:]).then_inc(sLDG, 16)
                    g.wait_ge(sLDG, ldg0 + 16)
                    g.dma_start(out=su_sb[:], in_=s_up[:]).then_inc(sLDG, 16)
                    for pc in range(NP):
                        if pc >= 2:
                            g.wait_ge(sMM, mm0 + 16 * 8 * (pc - 1))
                        g.wait_ge(sLDG, ldg0 + 32 + 32 * pc)
                        g.dma_start(out=e_wg[:, pc % 2],
                                    in_=wgate_v[:, :, pc * 1024:(pc + 1) * 1024]
                                    ).then_inc(sLDG, 16)
                        g.wait_ge(sLDG, ldg0 + 32 + 32 * pc + 16)
                        g.dma_start(out=e_wu[:, pc % 2],
                                    in_=wup_v[:, :, pc * 1024:(pc + 1) * 1024]
                                    ).then_inc(sLDG, 16)
                ctr["LDG"] += 32 + 32 * NP

                @block.sync
                def _(sy):
                    for pc in range(NP):
                        for ts in range(8):
                            idx = 8 * pc + ts
                            if idx >= 2:
                                sy.wait_ge(sMM, mm0 + 16 * (idx - 1))
                            sy.wait_ge(sLDS, lds0 + 16 * idx)
                            sy.dma_start(out=e_x[:, idx % 2],
                                         in_=xnT_v[:, :, ts * 512:(ts + 1) * 512]
                                         ).then_inc(sLDS, 16)
                ctr["LDS"] += 16 * 64
                ctr["STS"] += 32 * 512

                @block.tensor
                def _(t_):
                    for pc in range(NP):
                        t_.wait_ge(sLDG, ldg_e(pc))
                        for ts in range(8):
                            idx = 8 * pc + ts
                            t_.wait_ge(sLDS, lds0 + 16 * (idx + 1))
                            for g in range(16):
                                f = g // 2
                                gi = 16 * idx + g
                                if gi >= 4:
                                    i2, g2 = divmod(gi - 4, 16)
                                    f2, is_up = divmod(g2, 2)
                                    t_.wait_ge(sEVV, evv_e(i2, f2, 1 + is_up))
                                wsrc = e_wg if g % 2 == 0 else e_wu
                                ins = None
                                for k in range(KD):
                                    ins = t_.matmul(bank(g % 4),
                                                    wsrc[:, pc % 2, k, f * 128:(f + 1) * 128],
                                                    e_x[:, idx % 2, k, :],
                                                    start=(k == 0), stop=(k == KD - 1))
                                ins.then_inc(sMM, 1)
                ctr["MM"] += 16 * 64

                @block.scalar
                def _(sc):
                    def fstore(sf):
                        i2, f2 = divmod(sf, 8)
                        pc2, ts2 = divmod(i2, 8)
                        kf = pc2 * 8 + f2
                        sc.wait_ge(sEVV, evv_e(i2, f2, 2))
                        sc.wait_ge(sSTS, sts_e(sf) - 32)
                        sc.dma_start(out=fuse_d[2 * ts2, :, kf, :],
                                     in_=e_fstg[:, sf % 4, 0:256]).then_inc(sSTS, 16)
                        sc.wait_ge(sSTS, sts_e(sf) - 16)
                        sc.dma_start(out=fuse_d[2 * ts2 + 1, :, kf, :],
                                     in_=e_fstg[:, sf % 4, 256:512]).then_inc(sSTS, 16)
                    for idx in range(64):
                        for f in range(8):
                            sc.wait_ge(sEVV, evv_e(idx, f, 1))
                            sf = 8 * idx + f
                            if sf >= 2:
                                i2, f2 = divmod(sf - 2, 8)
                                sc.wait_ge(sEVV, evv_e(i2, f2, 2))
                            sc.activation(e_tg2[:, f % 2], e_tg[:, f % 2],
                                          AF.Silu).then_inc(sEVS, 1)
                            if sf >= 2:
                                fstore(sf - 2)
                    fstore(510)
                    fstore(511)
                ctr["EVS"] += 8 * 64

                @block.vector
                def _(v):
                    for idx in range(64):
                        pc = idx // 8
                        for f in range(8):
                            sf = 8 * idx + f
                            c = pc * 8 + f
                            v.wait_ge(sMM, mm_e(idx, 2 * f))
                            if sf >= 2:
                                i2, f2 = divmod(sf - 2, 8)
                                v.wait_ge(sEVS, evs_e(i2, f2))
                            v.tensor_scalar(e_tg[:, f % 2], bank((2 * f) % 4),
                                            sg_sb[:, c:c + 1], None,
                                            AluOpType.mult).then_inc(sEVV, 1)
                            v.wait_ge(sMM, mm_e(idx, 2 * f + 1))
                            v.tensor_scalar(e_tu[:, f % 2], bank((2 * f + 1) % 4),
                                            su_sb[:, c:c + 1], None,
                                            AluOpType.mult).then_inc(sEVV, 1)
                            v.wait_ge(sEVV, evv_e(idx, f, 1))
                            v.wait_ge(sEVS, evs_e(idx, f))
                            if sf >= 4:
                                v.wait_ge(sSTS, sts_e(sf - 4))
                            v.tensor_tensor(e_fstg[:, sf % 4], e_tg2[:, f % 2],
                                            e_tu[:, f % 2], AluOpType.mult).then_inc(sEVV, 1)
                ctr["EVV"] += 24 * 64

            barrier()

        if upto_i >= 5:
            with (
                nc.sbuf_tensor("f_wd", [128, 64, 512], BF16) as f_wd,
                nc.sbuf_tensor("f_fu", [128, 2, 64, 256], BF16) as f_fu,
                nc.sbuf_tensor("f_sdq", [128, 512], F32) as f_sdq,
                nc.sbuf_tensor("sd_row", [1, D], F32) as sd_row,
                nc.sbuf_tensor("ones_sb", [1, 128], F32) as ones_sb,
                nc.sbuf_tensor("f_so", [128, 2, 512], F32) as f_so,
                nc.sbuf_tensor("f_x", [128, 2, 512], F32) as f_x,
                nc.sbuf_tensor("f_t1", [128, 2, 512], F32) as f_t1,
                nc.sbuf_tensor("f_t2", [128, 2, 512], F32) as f_t2,
                nc.sbuf_tensor("f_os", [128, 2, 512], F32) as f_os,
            ):
                lds0, ldg0, mm0, evv0, evs0, sts0 = (ctr["LDS"], ctr["LDG"], ctr["MM"],
                                                     ctr["EVV"], ctr["EVS"], ctr["STS"])
                def mm_q(q):
                    return mm0 + 33 * q + 1

                def mm_g(q, gg):
                    return mm0 + 33 * q + 1 + gg + 1

                def evv_q(q):
                    return evv0 + 97 * q + 1

                def evv_g(q, gg, j):
                    return evv0 + 97 * q + 1 + 3 * gg + j

                evs_g = lambda q, gg: evs0 + 32 * q + gg + 1
                ldf0 = ctr["LDF"]
                lds_fu = lambda q, blk: ldf0 + 16 * (16 * q + blk + 1)
                lds_sx = lambda q, blk, i: lds0 + 1024 * q + 64 * blk + 32 * (i + 1)
                sts_g = lambda q, gg: sts0 + 16 * (32 * q + gg + 1)

                @block.gpsimd
                def _(g):
                    g.wait_ge(sLDG, ldg0)
                    g.dma_start(out=sd_row[:], in_=s_down[None, :]).then_inc(sLDG, 16)
                    g.wait_ge(sLDG, ldg0 + 16)
                    g.dma_start(out=ones_sb[:], in_=ones_in[:]).then_inc(sLDG, 16)
                    for q in range(4):
                        if q >= 1:
                            g.wait_ge(sMM, mm_g(q - 1, 31))
                        g.wait_ge(sLDG, ldg0 + 32 + 16 * q)
                        g.dma_start(out=f_wd[:],
                                    in_=wdown_v[:, :, q * 512:(q + 1) * 512]
                                    ).then_inc(sLDG, 16)
                ctr["LDG"] += 96

                @block.sync
                def _(sy):
                    def fuse_load(fidx):
                        q3, blk3 = divmod(fidx, 16)
                        if fidx >= 2:
                            pgi = 2 * fidx - 3
                            q2, g2 = divmod(pgi, 32)
                            sy.wait_ge(sMM, mm_g(q2, g2))
                        sy.wait_ge(sLDF, ldf0 + 16 * fidx)
                        sy.dma_start(out=f_fu[:, fidx % 2],
                                     in_=fuse_d[blk3]).then_inc(sLDF, 16)

                    def out_store(gi):
                        q2, g2 = divmod(gi, 32)
                        sy.wait_ge(sEVV, evv_g(q2, g2, 3))
                        sy.wait_ge(sSTS, sts_g(q2, g2) - 16)
                        blk2, i2 = divmod(g2, 2)
                        sy.dma_start(out=out_v[:, 2 * blk2 + i2, q2 * 512:(q2 + 1) * 512],
                                     in_=f_os[:, gi % 2]).then_inc(sSTS, 16)

                    fuse_load(0)
                    for q in range(4):
                        for blk in range(16):
                            fidx = 16 * q + blk
                            if fidx + 1 < 64:
                                fuse_load(fidx + 1)
                            for i in range(2):
                                gi = 32 * q + 2 * blk + i
                                if gi >= 2:
                                    q2, g2 = divmod(gi - 2, 32)
                                    sy.wait_ge(sEVV, evv_g(q2, g2, 3))
                                sy.wait_ge(sLDS, lds_sx(q, blk, i) - 32)
                                sy.dma_start(out=f_so[:, gi % 2],
                                             in_=sout_v[:, 2 * blk + i, q * 512:(q + 1) * 512]
                                             ).then_inc(sLDS, 16)
                                sy.wait_ge(sLDS, lds_sx(q, blk, i) - 16)
                                sy.dma_start(out=f_x[:, gi % 2],
                                             in_=x_v[:, 2 * blk + i, q * 512:(q + 1) * 512]
                                             ).then_inc(sLDS, 16)
                                if gi >= 2:
                                    out_store(gi - 2)
                    out_store(126)
                    out_store(127)
                    sy.wait_ge(sSTS, sts0 + 16 * 128)
                ctr["LDS"] += 4 * 16 * 64
                ctr["LDF"] += 16 * 64
                ctr["STS"] += 16 * 128

                @block.tensor
                def _(t_):
                    t_.wait_ge(sLDG, ldg0 + 32)
                    for q in range(4):
                        t_.wait_ge(sLDG, ldg0 + 32 + 16 * (q + 1))
                        if q >= 1:
                            t_.wait_ge(sEVV, evv_q(q - 1) + 96)
                        t_.matmul(bank(4), ones_sb[:], sd_row[:, q * 512:(q + 1) * 512],
                                  start=True, stop=True).then_inc(sMM, 1)
                        for blk in range(16):
                            t_.wait_ge(sLDF, lds_fu(q, blk))
                            for i in range(2):
                                gg = 2 * blk + i
                                gi = 32 * q + gg
                                if gi >= 4:
                                    q2, g2 = divmod(gi - 4, 32)
                                    t_.wait_ge(sEVV, evv_g(q2, g2, 1))
                                ins = None
                                for k in range(64):
                                    ins = t_.matmul(bank(gg % 4),
                                                    f_fu[:, blk % 2, k, i * 128:(i + 1) * 128],
                                                    f_wd[:, k, :],
                                                    start=(k == 0), stop=(k == 63))
                                ins.then_inc(sMM, 1)
                ctr["MM"] += 33 * 4

                @block.scalar
                def _(sc):
                    for q in range(4):
                        for gg in range(32):
                            sc.wait_ge(sEVV, evv_g(q, gg, 1))
                            gi = 32 * q + gg
                            if gi >= 2:
                                q2, g2 = divmod(gi - 2, 32)
                                sc.wait_ge(sEVV, evv_g(q2, g2, 2))
                            sc.activation(f_t2[:, gg % 2], f_t1[:, gg % 2],
                                          AF.Silu).then_inc(sEVS, 1)
                ctr["EVS"] += 32 * 4

                @block.vector
                def _(v):
                    for q in range(4):
                        v.wait_ge(sMM, mm_q(q))
                        v.tensor_copy(f_sdq[:], bank(4)).then_inc(sEVV, 1)
                        v.wait_ge(sEVV, evv_q(q))
                        for blk in range(16):
                            for i in range(2):
                                gg = 2 * blk + i
                                gi = 32 * q + gg
                                v.wait_ge(sMM, mm_g(q, gg))
                                if gi >= 2:
                                    q2, g2 = divmod(gi - 2, 32)
                                    v.wait_ge(sEVS, evs_g(q2, g2))
                                v.tensor_tensor(f_t1[:, gg % 2], bank(gg % 4), f_sdq[:],
                                                AluOpType.mult).then_inc(sEVV, 1)
                                v.wait_ge(sEVS, evs_g(q, gg))
                                v.wait_ge(sLDS, lds_sx(q, blk, i))
                                v.tensor_tensor(f_t1[:, gg % 2], f_t2[:, gg % 2],
                                                f_so[:, gi % 2], AluOpType.mult).then_inc(sEVV, 1)
                                v.wait_ge(sEVV, evv_g(q, gg, 2))
                                if gi >= 2:
                                    v.wait_ge(sSTS, sts_g(*divmod(gi - 2, 32)))
                                v.tensor_tensor(f_os[:, gi % 2], f_t1[:, gg % 2],
                                                f_x[:, gi % 2], AluOpType.add).then_inc(sEVV, 1)
                ctr["EVV"] += 97 * 4

        @block.sync
        def _(sy):
            sy.wait_ge(sSTS, ctr["STS"])
            sy.wait_ge(sLDS, ctr["LDS"])

        @block.gpsimd
        def _(g):
            g.wait_ge(sLDG, ctr["LDG"])

        @block.scalar
        def _(sc):
            sc.wait_ge(sLDK, ctr["LDK"])
            sc.wait_ge(sLDV, ctr["LDV"])
            sc.wait_ge(sLDF, ctr["LDF"])

    return nc


def host_inputs(inp, core):
    import ml_dtypes as mld
    x = np.ascontiguousarray(np.asarray(inp["L0_attn_x"])[core])
    m = {}
    m["x_in"] = x
    m["xt_in"] = np.ascontiguousarray(x.T)
    m["wq_t"] = np.ascontiguousarray(np.asarray(inp["L0_attn_Wq"]).T)
    m["wk_t"] = np.ascontiguousarray(np.asarray(inp["L0_attn_Wk"]).T)
    m["wv_t"] = np.ascontiguousarray(np.asarray(inp["L0_attn_Wv"]).T)
    m["wg_t"] = np.ascontiguousarray(np.asarray(inp["L0_attn_Wg"]).T)
    m["wp_t"] = np.ascontiguousarray(np.asarray(inp["L0_attn_Wp"]).T)
    m["w_gate"] = np.ascontiguousarray(np.asarray(inp["L0_ffn_w_gate"]))
    m["w_up"] = np.ascontiguousarray(np.asarray(inp["L0_ffn_w_up"]))
    m["w_down"] = np.ascontiguousarray(np.asarray(inp["L0_ffn_w_down"]))
    m["s_gate"] = np.ascontiguousarray(np.asarray(inp["L0_ffn_s_gate"]).reshape(64, 128).T)
    m["s_up"] = np.ascontiguousarray(np.asarray(inp["L0_ffn_s_up"]).reshape(64, 128).T)
    m["s_down"] = np.ascontiguousarray(np.asarray(inp["L0_ffn_s_down"]))
    m["ident_in"] = np.eye(128, dtype=mld.bfloat16)
    m["identf_in"] = np.eye(128, dtype=np.float32)
    m["ones_in"] = np.ones((1, 128), dtype=np.float32)
    mk = np.zeros((128, 512), dtype=np.float32)
    ii = np.arange(256)
    pp = np.arange(128)[:, None]
    mk[:, 0:256] = (ii[None, :] >= pp).astype(np.float32)
    mk[:, 256:512] = (ii[None, :] >= pp + 128).astype(np.float32)
    m["masks_in"] = mk
    return m


from concourse.bass_utils import run_bass_kernel_spmd


def kernel(**inputs):
    nc = build(upto="E2", debug=False)
    in_maps = [host_inputs(inputs, core) for core in range(8)]
    res = run_bass_kernel_spmd(nc, in_maps, core_ids=list(range(8)))
    out = np.stack([np.asarray(res.results[i]["out"]) for i in range(8)], axis=0)
    return out
